# version 17
# baseline (speedup 1.0000x reference)
import contextlib
import numpy as np
import concourse.bass as bass
import concourse.mybir as mybir

F32 = mybir.dt.float32
BF16 = mybir.dt.bfloat16
I16 = mybir.dt.int16
AF = mybir.ActivationFunctionType
ALU = mybir.AluOpType
AX = mybir.AxisListType

D = 1024
H = 8
KCH = 8
TC = 128
EPS = 1e-8

ENGINES = ("pe", "act", "dve", "pool", "sp")


def _name(x):
    if isinstance(x, tuple):
        h, r = x
    else:
        h, r = x, None
    nm = getattr(h, "name", None)
    if nm is None:
        t = getattr(h, "tensor", None)
        nm = getattr(t, "name", None) if t is not None else str(h)
    return nm, r


class Prog:
    def __init__(self, nc):
        self.nc = nc
        self.counts = {e: 0 for e in ENGINES}
        self.ops = {e: [] for e in ENGINES}
        self.last_write = {}
        self.readers = {}
        self.waited = {e: {} for e in ENGINES}
        self.n_waits = 0
        self.n_bumps = 0
        self.dma_streams = set()
        self.psum_names = set()

    def _ensure_stream(self, s):
        if s not in self.counts:
            self.counts[s] = 0
            self.waited[s] = {}
            self.dma_streams.add(s)

    def _tokens_for_read(self, key):
        nm, r = key
        d = self.last_write.get(nm)
        if not d:
            return []
        if r is None:
            return list(d.values())
        out = []
        if r in d:
            out.append(d[r])
        if None in d:
            out.append(d[None])
        return out

    def _tokens_for_write(self, key):
        toks = list(self._tokens_for_read(key))
        nm, r = key
        d = self.readers.get(nm)
        if d:
            if r is None:
                for lst in d.values():
                    toks.extend(lst)
            else:
                toks.extend(d.get(r, []))
                toks.extend(d.get(None, []))
        return toks

    def emit(self, eng, fn, reads=(), writes=(), tag=None):
        if eng.startswith("dma:"):
            stream, queue = eng[4:].split("@")
            stream = "dma_" + stream
            self._ensure_stream(stream)
            issue_eng, cnt_eng, inc = queue, stream, 16
        else:
            issue_eng, cnt_eng, inc = eng, eng, 1

        rkeys = [_name(x) for x in reads]
        wkeys = [_name(x) for x in writes]
        pr = [k for k in rkeys if k[0] in self.psum_names]
        if pr:
            rkeys = [k for k in rkeys if k[0] not in self.psum_names]
            wkeys = wkeys + pr
        need = {}

        def add_need(peng, pcnt):
            if peng == cnt_eng and cnt_eng in self.dma_streams:
                return
            if peng in self.dma_streams and pcnt < self.counts[peng]:
                pcnt = self.counts[peng]
                self.n_bumps += 1
            if self.waited[issue_eng].get(peng, 0) >= pcnt:
                return
            need[peng] = max(need.get(peng, 0), pcnt)

        if cnt_eng in self.dma_streams and self.counts[cnt_eng] > 0:
            if self.waited[issue_eng].get(cnt_eng, 0) < self.counts[cnt_eng]:
                need[cnt_eng] = self.counts[cnt_eng]
        for k in rkeys:
            for peng, pcnt in self._tokens_for_read(k):
                add_need(peng, pcnt)
        for k in wkeys:
            for peng, pcnt in self._tokens_for_write(k):
                add_need(peng, pcnt)
        for peng, pcnt in need.items():
            self.waited[issue_eng][peng] = max(self.waited[issue_eng].get(peng, 0), pcnt)
        self.n_waits += len(need)

        self.counts[cnt_eng] += inc
        tok = (cnt_eng, self.counts[cnt_eng])
        for nm, r in wkeys:
            d = self.last_write.setdefault(nm, {})
            rd = self.readers.setdefault(nm, {})
            if r is None:
                d.clear()
                rd.clear()
                d[None] = tok
            else:
                d[r] = tok
                rd.pop(r, None)
        for nm, r in rkeys:
            self.readers.setdefault(nm, {}).setdefault(r, []).append(tok)

        self.ops[issue_eng].append((dict(need), fn, cnt_eng, inc, tag))

    def barrier_token(self):
        return dict(self.counts)

    def finalize(self, es, end_waits=None):
        nc = self.nc
        sems = {}
        for e in self.counts:
            if self.counts[e] > 0 or e == "sp":
                sems[e] = es.enter_context(nc.semaphore(f"sem_{e}"))
        self.sems = sems
        block = es.enter_context(nc.Block())
        eng_attr = {"pe": "tensor", "act": "scalar", "dve": "vector",
                    "pool": "gpsimd", "sp": "sync"}

        def make_body(ename):
            def body(engine):
                for need, fn, cnt_eng, inc, tag in self.ops[ename]:
                    for peng, pcnt in sorted(need.items()):
                        engine.wait_ge(sems[peng], pcnt)
                    instr = fn(engine)
                    if instr is not None:
                        instr.then_inc(sems[cnt_eng], inc)
                if ename == "sp" and end_waits:
                    for peng, pcnt in sorted(end_waits.items()):
                        if pcnt > 0 and peng in sems:
                            engine.wait_ge(sems[peng], pcnt)
            return body

        for ename in ENGINES:
            if self.ops[ename] or (ename == "sp" and end_waits):
                getattr(block, eng_attr[ename])(make_body(ename))


def build_nc(S=2048, nseq=2):
    NCH = S // TC
    nc = bass.Bass("TRN2", target_bir_lowering=False, debug=False)

    dx = nc.dram_tensor("x", [nseq, S, D], F32, kind="ExternalInput").ap()
    dwq = nc.dram_tensor("wq", [D, 512], F32, kind="ExternalInput").ap()
    dwk = nc.dram_tensor("wk", [D, 512], F32, kind="ExternalInput").ap()
    dwal = nc.dram_tensor("wal", [D, 512], F32, kind="ExternalInput").ap()
    dwv = nc.dram_tensor("wv", [D, D], F32, kind="ExternalInput").ap()
    dwo = nc.dram_tensor("wo", [D, D], F32, kind="ExternalInput").ap()
    dwb = nc.dram_tensor("wb", [D, D], F32, kind="ExternalInput").ap()
    dwab = nc.dram_tensor("wab", [1, 512], F32, kind="ExternalInput").ap()
    dbb = nc.dram_tensor("bb", [1, D], F32, kind="ExternalInput").ap()
    drml = nc.dram_tensor("rml", [D, 16], F32, kind="ExternalInput").ap()
    dlb = nc.dram_tensor("lb", [64, D], F32, kind="ExternalInput").ap()
    drb = nc.dram_tensor("rb", [1, 8], F32, kind="ExternalInput").ap()
    dout = nc.dram_tensor("out", [nseq, S, D], F32, kind="ExternalOutput").ap()

    es = contextlib.ExitStack()
    pg = Prog(nc)
    with es:
        T = {}

        def sb(name, shp, dt=F32):
            T[name] = es.enter_context(nc.sbuf_tensor(name, shp, dt))
            return T[name]

        def ps(name, shp, dt=F32):
            T[name] = es.enter_context(nc.psum_tensor(name, shp, dt))
            pg.psum_names.add(name)
            return T[name]

        for w, cols in [("wq", 8 * 512), ("wk", 8 * 512), ("wal", 8 * 512),
                        ("wv", 8 * 1024), ("wo", 8 * 1024), ("wb", 8 * 1024)]:
            sb(w + "_h", [128, cols], BF16)
        sb("rm_h", [128, 128], BF16)
        sb("lb_h", [65, D], BF16)
        sb("lb_l", [65, D], BF16)
        sb("wab2", [2, 512], BF16)
        sb("rb2", [2, 16], BF16)
        sb("ones2", [2, 128], BF16)
        sb("idf", [128, 128], F32)
        sb("m0", [128, 128], F32)
        sb("m0r", [128, 512], F32)
        sb("id8", [8, 8], F32)
        sb("repe", [8, 64], F32)
        sb("repr", [8, 64], F32)
        for i in range(3):
            sb(f"xb{i}", [128, D], F32)
        sb("xnt_h", [128, D], BF16)
        sb("vh_h", [128, D], BF16)
        sb("yt_h", [128, D], BF16)
        for s in range(nseq):
            sb(f"nch{s}", [8, D], BF16)
            sb(f"ncsf{s}", [8, D], F32)
            sb(f"zlast{s}", [8, 1], F32)
        sb("xnf", [128, D], F32)
        sb("bsc", [128, D], F32)
        sb("wstf", [128, 128], F32)
        sb("wst_h", [128, 128], BF16)
        for n in ["atf", "eat", "ebt", "zt", "lnzt", "ptt", "pzt",
                  "cwt", "bmpt", "eet", "gtf", "mtf"]:
            sb(n, [8, 128], F32)
        sb("cwh", [8, 128], BF16)
        sb("eecol_h", [128, 8], BF16)
        sb("bmpcol", [128, 8], F32)
        sb("epend", [8, 1], F32)
        for n in ["xsum", "ssq", "mean", "m2", "var", "rstd", "mrstd",
                  "mx", "nmx", "sumg", "invg"]:
            sb(n, [128, 1], F32)
        sb("av", [128, 8], F32)
        sb("bq", [128, 8], F32)
        sb("gf", [128, 8], F32)
        sb("grf", [128, 8], F32)
        sb("midf", [128, 8], F32)
        sb("gbits", [128, 8], I16)
        sb("m1f", [64, 128], F32)
        sb("gff", [64, 128], F32)
        sb("g65_h", [65, 128], BF16)
        ps("t0", [128, 512])
        ps("t1", [128, 512])
        ps("pq", [128, 512])
        ps("pk", [128, 512])
        ps("pal", [128, 512])
        ps("pw", [128, 512])
        ps("pv", [128, 1024])

        E = pg.emit

        E("pool", lambda e: e.memset(T["idf"][:], 0.0), writes=[T["idf"]])
        E("pool", lambda e: e.affine_select(out=T["idf"][:], in_=T["idf"][:],
                                            compare_op=ALU.not_equal, fill=1.0, base=0,
                                            pattern=[[-1, 128]], channel_multiplier=1),
          reads=[T["idf"]], writes=[T["idf"]])
        E("pool", lambda e: e.memset(T["m0"][:], 0.0), writes=[T["m0"]])
        E("pool", lambda e: e.affine_select(out=T["m0"][:], in_=T["m0"][:],
                                            compare_op=ALU.is_ge, fill=-1e30, base=0,
                                            pattern=[[1, 128]], channel_multiplier=-1),
          reads=[T["m0"]], writes=[T["m0"]])
        E("pool", lambda e: e.memset(T["m0r"][:], 0.0), writes=[T["m0r"]])
        E("pool", lambda e: e.affine_select(
            out=T["m0r"].rearrange("p (h t) -> p h t", h=4),
            in_=T["m0r"].rearrange("p (h t) -> p h t", h=4),
            compare_op=ALU.is_ge, fill=-1e30, base=0,
            pattern=[[0, 4], [1, 128]], channel_multiplier=-1),
          reads=[T["m0r"]], writes=[T["m0r"]])
        E("pool", lambda e: e.memset(T["id8"][:], 0.0), writes=[T["id8"]])
        E("pool", lambda e: e.affine_select(out=T["id8"][:], in_=T["id8"][:],
                                            compare_op=ALU.not_equal, fill=1.0, base=0,
                                            pattern=[[-1, 8]], channel_multiplier=1),
          reads=[T["id8"]], writes=[T["id8"]])
        E("pool", lambda e: e.memset(T["repe"][:], 0.0), writes=[T["repe"]])
        E("pool", lambda e: e.affine_select(
            out=T["repe"].rearrange("p (a b) -> p a b", a=8),
            in_=T["repe"].rearrange("p (a b) -> p a b", a=8),
            compare_op=ALU.not_equal, fill=1.0, base=0,
            pattern=[[-1, 8], [0, 8]], channel_multiplier=1),
          reads=[T["repe"]], writes=[T["repe"]])
        E("pool", lambda e: e.memset(T["repr"][:], 0.0), writes=[T["repr"]])
        E("pool", lambda e: e.affine_select(
            out=T["repr"].rearrange("p (a b) -> p a b", a=8),
            in_=T["repr"].rearrange("p (a b) -> p a b", a=8),
            compare_op=ALU.not_equal, fill=1.0, base=0,
            pattern=[[0, 8], [-1, 8]], channel_multiplier=1),
          reads=[T["repr"]], writes=[T["repr"]])
        E("pool", lambda e: e.memset(T["ones2"][:], 1.0), writes=[T["ones2"]])
        E("pool", lambda e: e.memset(T["g65_h"][64:65, :], 1.0), writes=[(T["g65_h"], "b")])
        E("pool", lambda e: e.memset(T["rb2"][:], 0.0), writes=[T["rb2"]])
        for s in range(nseq):
            E("pool", lambda e, s=s: e.memset(T[f"nch{s}"][:], 0.0), writes=[T[f"nch{s}"]])
            E("pool", lambda e, s=s: e.memset(T[f"ncsf{s}"][:], 0.0), writes=[T[f"ncsf{s}"]])
            E("pool", lambda e, s=s: e.memset(T[f"zlast{s}"][:], 0.0), writes=[T[f"zlast{s}"]])

        tps = ["t0", "t1"]

        def emit_prep():
            widx = 0
            stgs = ["bsc", "xb2"]

            def prep_w(dw, dname, douts):
                nonlocal widx
                step = 1024 // douts
                for k0 in range(0, KCH, step):
                    stg = T[stgs[widx % 2]]
                    E(f"dma:wf{widx % 2}@act",
                      lambda e, dw=dw, k0=k0, stg=stg, step=step, douts=douts:
                      e.dma_start(
                          stg[:].rearrange("p (c j) -> p c j", c=step),
                          dw.rearrange("(c p) j -> p c j", p=128)[:, k0:k0 + step, :]
                          if step > 1 else dw[k0 * 128:(k0 + 1) * 128, :].unsqueeze(1)),
                      writes=[stg])
                    co = k0 * douts
                    E("act", lambda e, stg=stg, co=co, dname=dname:
                      e.activation(out=T[dname + "_h"][:, co:co + 1024], in_=stg[:],
                                   func=AF.Copy),
                      reads=[stg], writes=[T[dname + "_h"]])
                    widx += 1

            def prep_w16(dw, dname):
                nonlocal widx
                stg = T[stgs[widx % 2]]
                E(f"dma:wf{widx % 2}@act",
                  lambda e, dw=dw, stg=stg: e.dma_start(
                      stg[:].rearrange("p (c j) -> p c j", c=8)[:, :, 0:16],
                      dw.rearrange("(c p) j -> p c j", p=128)),
                  writes=[stg])
                E("act", lambda e, stg=stg, dname=dname:
                  e.activation(out=T[dname + "_h"][:],
                               in_=stg[:].rearrange("p (c j) -> p c j", c=8)[:, :, 0:16],
                               func=AF.Copy),
                  reads=[stg], writes=[T[dname + "_h"]])
                widx += 1

            prep_w(dwq, "wq", 512)
            prep_w(dwk, "wk", 512)
            prep_w(dwal, "wal", 512)
            prep_w(dwv, "wv", 1024)
            prep_w(dwb, "wb", 1024)
            prep_w(dwo, "wo", 1024)
            prep_w16(drml, "rm")
            E("dma:wf0@act", lambda e: e.dma_start(T["bsc"][0:64, :], dlb), writes=[T["bsc"]])
            E("act", lambda e: e.activation(out=T["lb_h"][0:64, :], in_=T["bsc"][0:64, :],
                                            func=AF.Copy),
              reads=[T["bsc"]], writes=[(T["lb_h"], "a")])
            E("dve", lambda e: e.tensor_tensor(T["lb_l"][0:64, :], T["bsc"][0:64, :],
                                               T["lb_h"][0:64, :], op=ALU.subtract),
              reads=[T["bsc"], (T["lb_h"], "a")], writes=[(T["lb_l"], "a")])
            E("dma:wf1@act", lambda e: e.dma_start(T["bsc"][64:65, :], dbb), writes=[T["bsc"]])
            E("dve", lambda e: e.tensor_copy(T["lb_h"][64:65, :], T["bsc"][64:65, :]),
              reads=[T["bsc"]], writes=[(T["lb_h"], "b")])
            E("dve", lambda e: e.tensor_tensor(T["lb_l"][64:65, :], T["bsc"][64:65, :],
                                               T["lb_h"][64:65, :], op=ALU.subtract),
              reads=[T["bsc"], (T["lb_h"], "b")], writes=[(T["lb_l"], "b")])
            E("dma:wf1@act", lambda e: e.dma_start(T["bsc"][0:1, 0:512], dwab), writes=[T["bsc"]])
            E("dve", lambda e: e.tensor_copy(T["vh_h"][0:1, 0:512], T["bsc"][0:1, 0:512]),
              reads=[T["bsc"]], writes=[T["vh_h"]])
            E("dve", lambda e: e.tensor_tensor(T["yt_h"][0:1, 0:512], T["bsc"][0:1, 0:512],
                                               T["vh_h"][0:1, 0:512], op=ALU.subtract),
              reads=[T["bsc"], T["vh_h"]], writes=[T["yt_h"]])
            E("dma:wr@act", lambda e: e.dma_start(T["wab2"][0:1, :], T["vh_h"][0:1, 0:512]),
              reads=[T["vh_h"]], writes=[T["wab2"]])
            E("dma:wr@act", lambda e: e.dma_start(T["wab2"][1:2, :], T["yt_h"][0:1, 0:512]),
              reads=[T["yt_h"]], writes=[T["wab2"]])
            E("dma:wf1@act", lambda e: e.dma_start(T["bsc"][0:1, 0:8], drb), writes=[T["bsc"]])
            E("dve", lambda e: e.tensor_copy(T["vh_h"][0:1, 0:8], T["bsc"][0:1, 0:8]),
              reads=[T["bsc"]], writes=[T["vh_h"]])
            E("dve", lambda e: e.tensor_tensor(T["yt_h"][0:1, 0:8], T["bsc"][0:1, 0:8],
                                               T["vh_h"][0:1, 0:8], op=ALU.subtract),
              reads=[T["bsc"], T["vh_h"]], writes=[T["yt_h"]])
            E("dma:wr@act", lambda e: e.dma_start(T["rb2"][0:1, 0:8], T["vh_h"][0:1, 0:8]),
              reads=[T["vh_h"]], writes=[T["rb2"]])
            E("dma:wr@act", lambda e: e.dma_start(T["rb2"][1:2, 0:8], T["yt_h"][0:1, 0:8]),
              reads=[T["yt_h"]], writes=[T["rb2"]])

        slots = [(s, ch) for ch in range(NCH) for s in range(nseq)]
        NS = len(slots)

        def load_slot(si):
            if si >= NS:
                return
            s, ch = slots[si]
            xt = T[f"xb{si % 3}"]
            E(f"dma:x{si % 3}@sp",
              lambda e, s=s, ch=ch, xt=xt: e.dma_start(xt[:], dx[s, ch * TC:(ch + 1) * TC, :]),
              writes=[xt])

        def xnth(k):
            return T["xnt_h"][:, k * 128:(k + 1) * 128]

        def proj_group(dest_ap, dest_res, wname, co_fn, nblk, extra=None,
                       extra_reads=()):
            wh = T[wname + "_h"]

            def g(e):
                last = None
                for k in range(KCH):
                    co = co_fn(k)
                    last = e.matmul(dest_ap, xnth(k), wh[:, co:co + nblk],
                                    start=(k == 0),
                                    stop=(k == KCH - 1 and extra is None))
                if extra is not None:
                    last = extra(e)
                return last
            reads = [T["xnt_h"], wh] + list(extra_reads)
            E("pe", g, reads=reads, writes=[dest_res])

        SBANKS = ("pal", "pw")

        def front_ln(si):
            if si >= NS:
                return
            s, ch = slots[si]
            xt = T[f"xb{si % 3}"]
            load_slot(si + 1)
            E("dve", lambda e, xt=xt: e.tensor_reduce(T["xsum"][:], xt[:], axis=AX.X, op=ALU.add),
              reads=[xt], writes=[T["xsum"]])
            E("act", lambda e, xt=xt: e.activation(out=T["xnf"][:], in_=xt[:], func=AF.Square,
                                                   accum_out=T["ssq"][:]),
              reads=[xt], writes=[T["xnf"], T["ssq"]])
            E("dve", lambda e: e.tensor_scalar_mul(T["mean"][:], T["xsum"][:], 1.0 / D),
              reads=[T["xsum"]], writes=[T["mean"]])
            E("pool", lambda e: e.tensor_tensor(T["m2"][:], T["mean"][:], T["mean"][:], op=ALU.mult),
              reads=[T["mean"]], writes=[T["m2"]])
            E("dve", lambda e: e.scalar_tensor_tensor(T["var"][:], T["ssq"][:], 1.0 / D,
                                                      T["m2"][:], op0=ALU.mult, op1=ALU.subtract),
              reads=[T["ssq"], T["m2"]], writes=[T["var"]])
            E("dve", lambda e: e.tensor_scalar_add(T["var"][:], T["var"][:], 1e-5),
              reads=[T["var"]], writes=[T["var"]])
            E("act", lambda e: e.activation(out=T["rstd"][:], in_=T["var"][:], func=AF.Ln),
              reads=[T["var"]], writes=[T["rstd"]])
            E("act", lambda e: e.activation(out=T["rstd"][:], in_=T["rstd"][:], func=AF.Exp, scale=-0.5),
              reads=[T["rstd"]], writes=[T["rstd"]])
            E("pool", lambda e: e.tensor_tensor(T["mrstd"][:], T["mean"][:], T["rstd"][:], op=ALU.mult),
              reads=[T["mean"], T["rstd"]], writes=[T["mrstd"]])
            E("dve", lambda e, xt=xt: e.tensor_scalar(T["xnf"][:], xt[:], T["rstd"][:], T["mrstd"][:],
                                                      op0=ALU.mult, op1=ALU.subtract),
              reads=[xt, T["rstd"], T["mrstd"]], writes=[T["xnf"]])

        def front_tr_mm(si):
            if si >= NS:
                return
            for half in range(2):
                tp = T[tps[half]]

                def tr4(e, half=half, tp=tp):
                    last = None
                    for j in range(4):
                        k = half * 4 + j
                        last = e.transpose(tp[:, j * 128:(j + 1) * 128],
                                           T["xnf"][:, k * 128:(k + 1) * 128], T["idf"][:])
                    return last
                E("pe", tr4, reads=[T["xnf"], T["idf"]], writes=[tp])

        def front_tr_evac(si):
            if si >= NS:
                return
            for half in range(2):
                tp = T[tps[half]]
                co = half * 512
                E("act", lambda e, tp=tp, co=co: e.activation(
                    out=T["xnt_h"][:, co:co + 512], in_=tp[:], func=AF.Copy),
                  reads=[tp], writes=[(T["xnt_h"], f"h{half}")])

        def mid(si):
            s, ch = slots[si]
            xt = T[f"xb{si % 3}"]
            nch_, ncf = T[f"nch{s}"], T[f"ncsf{s}"]
            zlast = T[f"zlast{s}"]

            proj_group(T["t0"][:, 0:16], T["t0"], "rm", lambda k: k * 16, 16,
                       extra=lambda e: e.matmul(T["t0"][:, 0:16], T["ones2"][:], T["rb2"][:],
                                                start=False, stop=True),
                       extra_reads=[T["ones2"], T["rb2"]])
            for part in range(2):
                def qmm(e, part=part):
                    last = None
                    for k in range(part * 4, part * 4 + 4):
                        co = k * 512
                        last = e.matmul(T["pq"][:], xnth(k), T["wq_h"][:, co:co + 512],
                                        start=(k == 0), stop=(k == KCH - 1))
                    return last
                E("pe", qmm,
                  reads=[(T["xnt_h"], f"h{part}"), T["wq_h"]],
                  writes=[T["pq"]])
            proj_group(T["pk"][:], T["pk"], "wk", lambda k: k * 512, 512)
            proj_group(T["pal"][:], T["pal"], "wal", lambda k: k * 512, 512,
                       extra=lambda e: e.matmul(T["pal"][:], T["ones2"][:], T["wab2"][:],
                                                start=False, stop=True),
                       extra_reads=[T["ones2"], T["wab2"]])

            E("dve", lambda e: e.tensor_copy(T["xnf"][:, 0:512], T["pq"][:]),
              reads=[T["pq"]], writes=[(T["xnf"], "sa")])
            E("dve", lambda e: e.tensor_tensor(T["xnf"][:, 512:1024], T["xnf"][:, 0:512],
                                               T["pk"][:], op=ALU.mult),
              reads=[(T["xnf"], "sa"), T["pk"]], writes=[(T["xnf"], "sb")])
            E("dve", lambda e: e.tensor_reduce(
                T["bq"][:].unsqueeze(2),
                T["xnf"][:, 512:1024].rearrange("p (h w) -> p h w", h=H),
                axis=AX.X, op=ALU.add),
              reads=[(T["xnf"], "sb")], writes=[T["bq"]])

            E("dve", lambda e: e.tensor_reduce(T["nmx"][:], T["t0"][:, 0:8], axis=AX.X, op=ALU.max,
                                               negate=True),
              reads=[T["t0"]], writes=[T["nmx"]])
            E("act", lambda e: e.activation(out=T["gf"][:], in_=T["t0"][:, 0:8], func=AF.Exp,
                                            bias=T["nmx"][:]),
              reads=[T["t0"], T["nmx"]], writes=[T["gf"]])
            E("dve", lambda e: e.tensor_reduce(T["sumg"][:], T["gf"][:], axis=AX.X, op=ALU.add),
              reads=[T["gf"]], writes=[T["sumg"]])
            E("dve", lambda e: e.reciprocal(T["invg"][:], T["sumg"][:]),
              reads=[T["sumg"]], writes=[T["invg"]])
            E("dve", lambda e: e.tensor_scalar(T["gf"][:], T["gf"][:], T["invg"][:], None,
                                               op0=ALU.mult),
              reads=[T["gf"], T["invg"]], writes=[T["gf"]])
            E("dve", lambda e: e.tensor_copy(T["gbits"][:].bitcast(BF16), T["gf"][:]),
              reads=[T["gf"]], writes=[T["gbits"]])
            E("dve", lambda e: e.tensor_copy(T["grf"][:], T["gbits"][:].bitcast(BF16)),
              reads=[T["gbits"]], writes=[T["grf"]])
            E("dve", lambda e: e.tensor_scalar_mul(T["midf"][:], T["t0"][:, 8:16], 2.0),
              reads=[T["t0"]], writes=[T["midf"]])

            proj_group(T["pv"][:, 0:512], (T["pv"], "h0"), "wv", lambda k: k * 1024, 512)
            proj_group(T["pv"][:, 512:1024], (T["pv"], "h1"), "wv", lambda k: k * 1024 + 512, 512)

            E("act", lambda e: e.activation(out=T["xnf"][:, 0:512], in_=T["pal"][:], func=AF.Exp),
              reads=[T["pal"]], writes=[(T["xnf"], "sa")])
            E("act", lambda e: e.activation(out=T["xnf"][:, 0:512], in_=T["xnf"][:, 0:512],
                                            func=AF.Ln, bias=1.0),
              reads=[(T["xnf"], "sa")], writes=[(T["xnf"], "sa")])
            E("dve", lambda e: e.tensor_reduce(
                T["av"][:].unsqueeze(2),
                T["xnf"][:, 0:512].rearrange("p (h w) -> p h w", h=H),
                axis=AX.X, op=ALU.add, negate=True),
              reads=[(T["xnf"], "sa")], writes=[T["av"]])

            E("act", lambda e: e.activation(out=T["vh_h"][:, 0:512], in_=T["pv"][:, 0:512],
                                            func=AF.Copy),
              reads=[(T["pv"], "h0")], writes=[(T["vh_h"], "h0")])
            E("dve", lambda e: e.tensor_copy(T["vh_h"][:, 512:1024], T["pv"][:, 512:1024]),
              reads=[(T["pv"], "h1")], writes=[(T["vh_h"], "h1")])
            front_ln(si + 1)
            E("pe", lambda e: e.transpose(T["pw"][0:8, 0:128], T["bq"][:], T["idf"][:]),
              reads=[T["bq"], T["idf"]], writes=[T["pw"]])
            E("act", lambda e: e.activation(out=T["ebt"][:], in_=T["pw"][0:8, 0:128], func=AF.Exp,
                                            scale=0.125),
              reads=[T["pw"]], writes=[T["ebt"]])
            E("pe", lambda e: e.transpose(T["t1"][0:8, 0:128], T["av"][:], T["idf"][:]),
              reads=[T["av"], T["idf"]], writes=[T["t1"]])
            E("dve", lambda e: e.tensor_copy(T["atf"][:], T["t1"][0:8, 0:128]),
              reads=[T["t1"]], writes=[T["atf"]])
            E("act", lambda e: e.activation(out=T["eat"][:], in_=T["t1"][0:8, 0:128], func=AF.Exp),
              reads=[T["t1"]], writes=[T["eat"]])
            E("pe", lambda e: e.transpose(T["t1"][0:8, 128:256], T["grf"][:], T["idf"][:]),
              reads=[T["grf"], T["idf"]], writes=[T["t1"]])
            E("dve", lambda e: e.tensor_copy(T["gtf"][:], T["t1"][0:8, 128:256]),
              reads=[T["t1"]], writes=[T["gtf"]])
            E("pe", lambda e: e.transpose(T["t0"][0:8, 128:256], T["midf"][:], T["idf"][:]),
              reads=[T["midf"], T["idf"]], writes=[T["t0"]])
            E("dve", lambda e: e.tensor_copy(T["mtf"][:], T["t0"][0:8, 128:256]),
              reads=[T["t0"]], writes=[T["mtf"]])
            E("pe", lambda e: e.matmul(T["t1"][0:64, 256:384], T["repe"][:], T["gtf"][:],
                                       start=True, stop=True),
              reads=[T["repe"], T["gtf"]], writes=[T["t1"]])
            E("dve", lambda e: e.tensor_copy(T["m1f"][:], T["t1"][0:64, 256:384]),
              reads=[T["t1"]], writes=[T["m1f"]])
            E("pe", lambda e: e.matmul(T["t0"][0:64, 256:384], T["repr"][:], T["mtf"][:],
                                       start=True, stop=True),
              reads=[T["repr"], T["mtf"]], writes=[T["t0"]])
            E("dve", lambda e: e.tensor_tensor(T["gff"][:], T["m1f"][:], T["t0"][0:64, 256:384],
                                               op=ALU.mult),
              reads=[T["m1f"], T["t0"]], writes=[T["gff"]])
            E("dve", lambda e: e.tensor_copy(T["g65_h"][0:64, :], T["gff"][:]),
              reads=[T["gff"]], writes=[(T["g65_h"], "a")])

            for hf in range(2):
                def umm(e, hf=hf):
                    dst = T["pv"][:, hf * 512:(hf + 1) * 512]
                    for k in range(KCH):
                        co = k * 1024 + hf * 512
                        e.matmul(dst, xnth(k), T["wb_h"][:, co:co + 512],
                                 start=(k == 0), stop=False)
                    sl = slice(hf * 512, (hf + 1) * 512)
                    e.matmul(dst, T["g65_h"][:], T["lb_h"][:, sl], start=False, stop=False)
                    return e.matmul(dst, T["g65_h"][:], T["lb_l"][:, sl], start=False, stop=True)
                E("pe", umm,
                  reads=[T["xnt_h"], T["wb_h"],
                         T["g65_h"], T["lb_h"], T["lb_l"]],
                  writes=[(T["pv"], f"h{hf}")])

            E("dve", lambda e: e.tensor_tensor_scan(T["ptt"][:], T["atf"][:], T["atf"][:], 0.0,
                                                    op0=ALU.add, op1=ALU.bypass),
              reads=[T["atf"]], writes=[T["ptt"]])
            E("dve", lambda e, zlast=zlast: e.tensor_tensor_scan(
                T["zt"][:], T["eat"][:], T["ebt"][:], zlast[:], op0=ALU.mult, op1=ALU.add),
              reads=[T["eat"], T["ebt"], zlast], writes=[T["zt"]])
            E("dve", lambda e, zlast=zlast: e.tensor_copy(zlast[:], T["zt"][:, 127:128]),
              reads=[T["zt"]], writes=[zlast])
            E("act", lambda e: e.activation(out=T["lnzt"][:], in_=T["zt"][:], func=AF.Ln),
              reads=[T["zt"]], writes=[T["lnzt"]])
            E("pool", lambda e: e.tensor_tensor(T["pzt"][:], T["ptt"][:], T["lnzt"][:],
                                                op=ALU.subtract),
              reads=[T["ptt"], T["lnzt"]], writes=[T["pzt"]])
            E("act", lambda e: e.activation(out=T["cwt"][:], in_=T["pzt"][:], func=AF.Exp),
              reads=[T["pzt"]], writes=[T["cwt"]])
            E("pool", lambda e: e.tensor_copy(T["cwh"][:], T["cwt"][:]),
              reads=[T["cwt"]], writes=[T["cwh"]])
            E("dve", lambda e: e.scalar_tensor_tensor(T["bmpt"][:], T["pw"][0:8, 0:128], 0.125,
                                                      T["ptt"][:], op0=ALU.mult,
                                                      op1=ALU.subtract),
              reads=[T["pw"], T["ptt"]], writes=[T["bmpt"]])
            E("act", lambda e: e.activation(out=T["eet"][:], in_=T["bmpt"][:], func=AF.Exp,
                                            bias=T["ptt"][:, 127:128]),
              reads=[T["bmpt"], T["ptt"]], writes=[T["eet"]])
            E("act", lambda e: e.activation(out=T["epend"][:], in_=T["ptt"][:, 127:128],
                                            func=AF.Exp),
              reads=[T["ptt"]], writes=[T["epend"]])
            E("pe", lambda e: e.transpose(T["pq"][0:128, 0:8], T["eet"][:], T["id8"][:]),
              reads=[T["eet"], T["id8"]], writes=[T["pq"]])
            E("act", lambda e: e.activation(out=T["eecol_h"][:], in_=T["pq"][:, 0:8], func=AF.Copy),
              reads=[T["pq"]], writes=[T["eecol_h"]])

            for hb in range(2):
                Sb = T["pq"] if hb == 0 else T["pk"]

                def selmm(e, Sb=Sb, hb=hb):
                    last = None
                    for i in range(4):
                        h = hb * 4 + i
                        e.matmul(Sb[:, i * 128:(i + 1) * 128],
                                 T["id8"][:, h:h + 1].broadcast_to([8, 128]),
                                 T["pzt"][:], start=True, stop=False)
                        last = e.matmul(Sb[:, i * 128:(i + 1) * 128],
                                        T["bmpt"][:],
                                        T["id8"][:, h:h + 1].broadcast_to([8, 128]),
                                        start=False, stop=True)
                    return last
                E("pe", selmm, reads=[T["id8"], T["pzt"], T["bmpt"]], writes=[Sb])
            for hb in range(2):
                Sb = T["pq"] if hb == 0 else T["pk"]
                E("dve", lambda e, Sb=Sb, hb=hb: e.tensor_tensor(
                    T["bsc"][:, hb * 512:(hb + 1) * 512], Sb[:], T["m0r"][:], op=ALU.add),
                  reads=[Sb, T["m0r"]], writes=[T["bsc"]])
            E("act", lambda e: e.activation(out=T["bsc"][:], in_=T["bsc"][:], func=AF.Exp),
              reads=[T["bsc"]], writes=[T["bsc"]])
            E("pool", lambda e: e.tensor_tensor(T["bsc"][:, 0:512], T["bsc"][:, 0:512],
                                                T["bsc"][:, 512:1024], op=ALU.add),
              reads=[T["bsc"]], writes=[T["bsc"]])
            E("dve", lambda e: e.tensor_reduce(
                T["wstf"][:].unsqueeze(2),
                T["bsc"][:, 0:512].rearrange("p (a t) -> p t a", a=4),
                axis=AX.X, op=ALU.add),
              reads=[T["bsc"]], writes=[T["wstf"]])
            E("act", lambda e: e.activation(out=T["wst_h"][:], in_=T["wstf"][:], func=AF.Copy),
              reads=[T["wstf"]], writes=[T["wst_h"]])

            for hf, bank in [(0, "pq"), (1, "pk")]:
                def nemm(e, hf=hf, bank=bank):
                    dst = T[bank][0:8, 0:512]
                    sl = slice(hf * 512, (hf + 1) * 512)
                    return e.matmul(dst, T["eecol_h"][:], T["vh_h"][:, sl],
                                    start=True, stop=True)
                E("pe", nemm,
                  reads=[T["eecol_h"], T["vh_h"]],
                  writes=[T[bank]])

            for bi, bank in [(0, "pal"), (1, "pw")]:
                def ycar(e, bi=bi, bank=bank):
                    last = None
                    for i in range(4):
                        db = bi * 4 + i
                        dst = T[bank][:, i * 128:(i + 1) * 128]
                        sl = slice(db * 128, (db + 1) * 128)
                        last = e.matmul(dst, nch_[0:8, sl], T["cwh"][:],
                                        start=(i == 0), stop=False)
                    return last
                E("pe", ycar,
                  reads=[T["cwh"], nch_], writes=[T[bank]])
            front_tr_mm(si + 1)
            for bi, bank in [(0, "pal"), (1, "pw")]:
                def yws(e, bi=bi, bank=bank):
                    last = None
                    for i in range(4):
                        db = bi * 4 + i
                        dst = T[bank][:, i * 128:(i + 1) * 128]
                        sl = slice(db * 128, (db + 1) * 128)
                        last = e.matmul(dst, T["vh_h"][:, sl], T["wst_h"][:], start=False,
                                        stop=(i == 3))
                    return last
                E("pe", yws,
                  reads=[T["wst_h"], T["vh_h"]], writes=[T[bank]])

            E("act", lambda e: e.activation(
                out=T["yt_h"][:, 0:512], in_=T["pal"][:], func=AF.Copy),
              reads=[T["pal"]], writes=[(T["yt_h"], "h0")])
            E("dve", lambda e: e.tensor_copy(T["yt_h"][:, 512:1024], T["pw"][:]),
              reads=[T["pw"]], writes=[(T["yt_h"], "h1")])

            for hf, bank in [(0, SBANKS[0]), (1, SBANKS[1])]:
                for part in range(2):
                    def smm(e, hf=hf, bank=bank, part=part):
                        dst = T[bank][:]
                        last = None
                        for k in range(part * 4, part * 4 + 4):
                            co = k * 1024 + hf * 512
                            last = e.matmul(dst, T["yt_h"][:, k * 128:(k + 1) * 128],
                                            T["wo_h"][:, co:co + 512],
                                            start=(k == 0), stop=(k == KCH - 1))
                        return last
                    E("pe", smm,
                      reads=[(T["yt_h"], f"h{part}"), T["wo_h"]],
                      writes=[T[bank]])

            for hf, bank in [(0, "pq"), (1, "pk")]:
                E("dve", lambda e, hf=hf, bank=bank, ncf=ncf: e.scalar_tensor_tensor(
                    ncf[:, hf * 512:(hf + 1) * 512], ncf[:, hf * 512:(hf + 1) * 512],
                    T["epend"][:], T[bank][0:8, 0:512], op0=ALU.mult, op1=ALU.add),
                  reads=[(ncf, f"h{hf}"), T["epend"], T[bank]], writes=[(ncf, f"h{hf}")])
            E("pool", lambda e, nch_=nch_, ncf=ncf: e.tensor_copy(nch_[:], ncf[:]),
              reads=[(ncf, "h0"), (ncf, "h1")], writes=[nch_])


        def back(si):
            s, ch = slots[si]
            xt = T[f"xb{si % 3}"]
            E("act", lambda e: e.activation(out=T["bsc"][:], in_=T["pv"][:], func=AF.Silu),
              reads=[(T["pv"], "h0"), (T["pv"], "h1")], writes=[T["bsc"]])
            E("dve", lambda e: e.tensor_tensor(T["bsc"][:, 0:512], T["bsc"][:, 0:512],
                                               T[SBANKS[0]][:], op=ALU.mult),
              reads=[T["bsc"], T[SBANKS[0]]], writes=[T["bsc"]])
            E("dve", lambda e: e.tensor_tensor(T["bsc"][:, 512:1024], T["bsc"][:, 512:1024],
                                               T[SBANKS[1]][:], op=ALU.mult),
              reads=[T["bsc"], T[SBANKS[1]]], writes=[T["bsc"]])
            E("pool", lambda e, xt=xt: e.tensor_tensor(xt[:], T["bsc"][:], xt[:], op=ALU.add),
              reads=[T["bsc"], xt], writes=[xt])
            E(f"dma:o{si % 3}@sp",
              lambda e, s=s, ch=ch, xt=xt: e.dma_start(dout[s, ch * TC:(ch + 1) * TC, :], xt[:]),
              reads=[xt])

        load_slot(0)
        front_ln(0)
        front_tr_mm(0)
        front_tr_evac(0)
        emit_prep()
        for si in range(NS):
            mid(si)
            front_tr_evac(si + 1)
            back(si)

        pg.finalize(es, end_waits=pg.barrier_token())
    n_instr = sum(len(v) for v in pg.ops.values())
    return nc, {"n_instr": n_instr, "n_waits": pg.n_waits, "n_bumps": pg.n_bumps}


def stage_inputs(inputs, core, n_cores=8):
    x = inputs["L0_attn_x"]
    B = x.shape[0]
    per = B // n_cores
    return {
        "x": np.ascontiguousarray(x[core * per:(core + 1) * per]),
        "wq": np.ascontiguousarray(inputs["L0_attn_W_q"].T),
        "wk": np.ascontiguousarray(inputs["L0_attn_W_k"].T),
        "wal": np.ascontiguousarray(inputs["L0_attn_W_alpha_w"].T),
        "wv": np.ascontiguousarray(inputs["L0_attn_W_v"].T),
        "wo": np.ascontiguousarray(inputs["L0_attn_W_o"].T),
        "wb": np.ascontiguousarray(inputs["L0_ffn_base_w"].T),
        "wab": np.ascontiguousarray(inputs["L0_attn_W_alpha_b"].reshape(1, 512)),
        "bb": np.ascontiguousarray(inputs["L0_ffn_base_b"].reshape(1, D)),
        "rml": np.ascontiguousarray(
            np.concatenate([inputs["L0_ffn_router_w"].T,
                            inputs["L0_ffn_lora_A"]], axis=1)),
        "lb": np.ascontiguousarray(inputs["L0_ffn_lora_B"].reshape(64, D)),
        "rb": np.ascontiguousarray(inputs["L0_ffn_router_b"].reshape(1, 8)),
    }


_NC_CACHE = {}


def kernel(**inputs):
    from concourse.bass_utils import run_bass_kernel_spmd
    n_cores = 8
    if "nc" not in _NC_CACHE:
        _NC_CACHE["nc"] = build_nc(S=2048, nseq=2)[0]
    nc = _NC_CACHE["nc"]
    in_maps = [stage_inputs(inputs, c, n_cores=n_cores) for c in range(n_cores)]
    res = run_bass_kernel_spmd(nc, in_maps, core_ids=list(range(n_cores)))
    out = np.concatenate([r["out"] for r in res.results], axis=0)
    return np.ascontiguousarray(out.reshape(16, 2048, 1024), dtype=np.float32)


# version 18
# speedup vs baseline: 1.2293x; 1.2293x over previous
import contextlib
import numpy as np
import concourse.bass as bass
import concourse.mybir as mybir

F32 = mybir.dt.float32
BF16 = mybir.dt.bfloat16
I16 = mybir.dt.int16
AF = mybir.ActivationFunctionType
ALU = mybir.AluOpType
AX = mybir.AxisListType

D = 1024
H = 8
KCH = 8
TC = 128
EPS = 1e-8

ENGINES = ("pe", "act", "dve", "pool", "sp")


def _name(x):
    if isinstance(x, tuple):
        h, r = x
    else:
        h, r = x, None
    nm = getattr(h, "name", None)
    if nm is None:
        t = getattr(h, "tensor", None)
        nm = getattr(t, "name", None) if t is not None else str(h)
    return nm, r


class Prog:
    def __init__(self, nc):
        self.nc = nc
        self.counts = {e: 0 for e in ENGINES}
        self.ops = {e: [] for e in ENGINES}
        self.last_write = {}
        self.readers = {}
        self.waited = {e: {} for e in ENGINES}
        self.n_waits = 0
        self.n_bumps = 0
        self.dma_streams = set()
        self.psum_names = set()

    def _ensure_stream(self, s):
        if s not in self.counts:
            self.counts[s] = 0
            self.waited[s] = {}
            self.dma_streams.add(s)

    def _tokens_for_read(self, key):
        nm, r = key
        d = self.last_write.get(nm)
        if not d:
            return []
        if r is None:
            return list(d.values())
        out = []
        if r in d:
            out.append(d[r])
        if None in d:
            out.append(d[None])
        return out

    def _tokens_for_write(self, key):
        toks = list(self._tokens_for_read(key))
        nm, r = key
        d = self.readers.get(nm)
        if d:
            if r is None:
                for lst in d.values():
                    toks.extend(lst)
            else:
                toks.extend(d.get(r, []))
                toks.extend(d.get(None, []))
        return toks

    def emit(self, eng, fn, reads=(), writes=(), tag=None):
        if eng.startswith("dma:"):
            stream, queue = eng[4:].split("@")
            stream = "dma_" + stream
            self._ensure_stream(stream)
            issue_eng, cnt_eng, inc = queue, stream, 16
        else:
            issue_eng, cnt_eng, inc = eng, eng, 1

        rkeys = [_name(x) for x in reads]
        wkeys = [_name(x) for x in writes]
        pr = [k for k in rkeys if k[0] in self.psum_names]
        if pr:
            rkeys = [k for k in rkeys if k[0] not in self.psum_names]
            wkeys = wkeys + pr
        need = {}

        def add_need(peng, pcnt):
            if peng == cnt_eng and cnt_eng in self.dma_streams:
                return
            if peng in self.dma_streams and pcnt < self.counts[peng]:
                pcnt = self.counts[peng]
                self.n_bumps += 1
            if self.waited[issue_eng].get(peng, 0) >= pcnt:
                return
            need[peng] = max(need.get(peng, 0), pcnt)

        if cnt_eng in self.dma_streams and self.counts[cnt_eng] > 0:
            if self.waited[issue_eng].get(cnt_eng, 0) < self.counts[cnt_eng]:
                need[cnt_eng] = self.counts[cnt_eng]
        for k in rkeys:
            for peng, pcnt in self._tokens_for_read(k):
                add_need(peng, pcnt)
        for k in wkeys:
            for peng, pcnt in self._tokens_for_write(k):
                add_need(peng, pcnt)
        for peng, pcnt in need.items():
            self.waited[issue_eng][peng] = max(self.waited[issue_eng].get(peng, 0), pcnt)
        self.n_waits += len(need)

        self.counts[cnt_eng] += inc
        tok = (cnt_eng, self.counts[cnt_eng])
        for nm, r in wkeys:
            d = self.last_write.setdefault(nm, {})
            rd = self.readers.setdefault(nm, {})
            if r is None:
                d.clear()
                rd.clear()
                d[None] = tok
            else:
                d[r] = tok
                rd.pop(r, None)
        for nm, r in rkeys:
            self.readers.setdefault(nm, {}).setdefault(r, []).append(tok)

        self.ops[issue_eng].append((dict(need), fn, cnt_eng, inc, tag))

    def barrier_token(self):
        return dict(self.counts)

    def finalize(self, es, end_waits=None):
        nc = self.nc
        sems = {}
        for e in self.counts:
            if self.counts[e] > 0 or e == "sp":
                sems[e] = es.enter_context(nc.semaphore(f"sem_{e}"))
        self.sems = sems
        block = es.enter_context(nc.Block())
        eng_attr = {"pe": "tensor", "act": "scalar", "dve": "vector",
                    "pool": "gpsimd", "sp": "sync"}

        def make_body(ename):
            def body(engine):
                for need, fn, cnt_eng, inc, tag in self.ops[ename]:
                    for peng, pcnt in sorted(need.items()):
                        engine.wait_ge(sems[peng], pcnt)
                    instr = fn(engine)
                    if instr is not None:
                        instr.then_inc(sems[cnt_eng], inc)
                if ename == "sp" and end_waits:
                    for peng, pcnt in sorted(end_waits.items()):
                        if pcnt > 0 and peng in sems:
                            engine.wait_ge(sems[peng], pcnt)
            return body

        for ename in ENGINES:
            if self.ops[ename] or (ename == "sp" and end_waits):
                getattr(block, eng_attr[ename])(make_body(ename))


def build_nc(S=2048, nseq=2):
    NCH = S // TC
    nc = bass.Bass("TRN2", target_bir_lowering=False, debug=False)

    dx = nc.dram_tensor("x", [nseq, S, D], F32, kind="ExternalInput").ap()
    dwq = nc.dram_tensor("wq", [D, 512], F32, kind="ExternalInput").ap()
    dwk = nc.dram_tensor("wk", [D, 512], F32, kind="ExternalInput").ap()
    dwal = nc.dram_tensor("wal", [D, 512], F32, kind="ExternalInput").ap()
    dwv = nc.dram_tensor("wv", [D, D], F32, kind="ExternalInput").ap()
    dwo = nc.dram_tensor("wo", [D, D], F32, kind="ExternalInput").ap()
    dwb = nc.dram_tensor("wb", [D, D], F32, kind="ExternalInput").ap()
    dwab = nc.dram_tensor("wab", [1, 512], F32, kind="ExternalInput").ap()
    dbb = nc.dram_tensor("bb", [1, D], F32, kind="ExternalInput").ap()
    drml = nc.dram_tensor("rml", [D, 16], F32, kind="ExternalInput").ap()
    dlb = nc.dram_tensor("lb", [64, D], F32, kind="ExternalInput").ap()
    drb = nc.dram_tensor("rb", [1, 8], F32, kind="ExternalInput").ap()
    dout = nc.dram_tensor("out", [nseq, S, D], F32, kind="ExternalOutput").ap()

    es = contextlib.ExitStack()
    pg = Prog(nc)
    with es:
        T = {}

        def sb(name, shp, dt=F32):
            T[name] = es.enter_context(nc.sbuf_tensor(name, shp, dt))
            return T[name]

        def ps(name, shp, dt=F32):
            T[name] = es.enter_context(nc.psum_tensor(name, shp, dt))
            pg.psum_names.add(name)
            return T[name]

        for w, cols in [("wq", 8 * 512), ("wk", 8 * 512), ("wal", 8 * 512),
                        ("wv", 8 * 1024), ("wo", 8 * 1024), ("wb", 8 * 1024)]:
            sb(w + "_h", [128, cols], BF16)
        sb("rm_h", [128, 128], BF16)
        sb("lb_h", [65, D], BF16)
        sb("lb_l", [65, D], BF16)
        sb("wab2", [2, 512], BF16)
        sb("rb2", [2, 16], BF16)
        sb("ones2", [2, 128], BF16)
        sb("idf", [128, 128], F32)
        sb("m0", [128, 128], F32)
        sb("m0r", [128, 512], F32)
        sb("id8", [8, 8], F32)
        sb("repe", [8, 64], F32)
        sb("repr", [8, 64], F32)
        for i in range(3):
            sb(f"xb{i}", [128, D], F32)
        sb("xnt_h", [128, D], BF16)
        sb("vh_h", [128, D], BF16)
        sb("yt_h", [128, D], BF16)
        for s in range(nseq):
            sb(f"nch{s}", [8, D], BF16)
            sb(f"ncsf{s}", [8, D], F32)
            sb(f"zlast{s}", [8, 1], F32)
        sb("xnf", [128, D], F32)
        sb("bsc", [128, D], F32)
        sb("wstf", [128, 128], F32)
        sb("wst_h", [128, 128], BF16)
        for n in ["atf", "eat", "ebt", "zt", "lnzt", "ptt", "pzt",
                  "cwt", "bmpt", "eet", "gtf", "mtf"]:
            sb(n, [8, 128], F32)
        sb("cwh", [8, 128], BF16)
        sb("eecol_h", [128, 8], BF16)
        sb("bmpcol", [128, 8], F32)
        sb("epend", [8, 1], F32)
        for n in ["xsum", "ssq", "mean", "m2", "var", "rstd", "mrstd",
                  "mx", "nmx", "sumg", "invg"]:
            sb(n, [128, 1], F32)
        sb("av", [128, 8], F32)
        sb("bq", [128, 8], F32)
        sb("gf", [128, 8], F32)
        sb("grf", [128, 8], F32)
        sb("midf", [128, 8], F32)
        sb("gbits", [128, 8], I16)
        sb("m1f", [64, 128], F32)
        sb("gff", [64, 128], F32)
        sb("g65_h", [65, 128], BF16)
        ps("t0", [128, 512])
        ps("t1", [128, 512])
        ps("pq", [128, 512])
        ps("pk", [128, 512])
        ps("pal", [128, 512])
        ps("pw", [128, 512])
        ps("pv", [128, 1024])

        E = pg.emit

        E("pool", lambda e: e.memset(T["idf"][:], 0.0), writes=[T["idf"]])
        E("pool", lambda e: e.affine_select(out=T["idf"][:], in_=T["idf"][:],
                                            compare_op=ALU.not_equal, fill=1.0, base=0,
                                            pattern=[[-1, 128]], channel_multiplier=1),
          reads=[T["idf"]], writes=[T["idf"]])
        E("pool", lambda e: e.memset(T["m0"][:], 0.0), writes=[T["m0"]])
        E("pool", lambda e: e.affine_select(out=T["m0"][:], in_=T["m0"][:],
                                            compare_op=ALU.is_ge, fill=-1e30, base=0,
                                            pattern=[[1, 128]], channel_multiplier=-1),
          reads=[T["m0"]], writes=[T["m0"]])
        E("pool", lambda e: e.memset(T["m0r"][:], 0.0), writes=[T["m0r"]])
        E("pool", lambda e: e.affine_select(
            out=T["m0r"].rearrange("p (h t) -> p h t", h=4),
            in_=T["m0r"].rearrange("p (h t) -> p h t", h=4),
            compare_op=ALU.is_ge, fill=-1e30, base=0,
            pattern=[[0, 4], [1, 128]], channel_multiplier=-1),
          reads=[T["m0r"]], writes=[T["m0r"]])
        E("pool", lambda e: e.memset(T["id8"][:], 0.0), writes=[T["id8"]])
        E("pool", lambda e: e.affine_select(out=T["id8"][:], in_=T["id8"][:],
                                            compare_op=ALU.not_equal, fill=1.0, base=0,
                                            pattern=[[-1, 8]], channel_multiplier=1),
          reads=[T["id8"]], writes=[T["id8"]])
        E("pool", lambda e: e.memset(T["repe"][:], 0.0), writes=[T["repe"]])
        E("pool", lambda e: e.affine_select(
            out=T["repe"].rearrange("p (a b) -> p a b", a=8),
            in_=T["repe"].rearrange("p (a b) -> p a b", a=8),
            compare_op=ALU.not_equal, fill=1.0, base=0,
            pattern=[[-1, 8], [0, 8]], channel_multiplier=1),
          reads=[T["repe"]], writes=[T["repe"]])
        E("pool", lambda e: e.memset(T["repr"][:], 0.0), writes=[T["repr"]])
        E("pool", lambda e: e.affine_select(
            out=T["repr"].rearrange("p (a b) -> p a b", a=8),
            in_=T["repr"].rearrange("p (a b) -> p a b", a=8),
            compare_op=ALU.not_equal, fill=1.0, base=0,
            pattern=[[0, 8], [-1, 8]], channel_multiplier=1),
          reads=[T["repr"]], writes=[T["repr"]])
        E("pool", lambda e: e.memset(T["ones2"][:], 1.0), writes=[T["ones2"]])
        E("pool", lambda e: e.memset(T["g65_h"][64:65, :], 1.0), writes=[(T["g65_h"], "b")])
        E("pool", lambda e: e.memset(T["rb2"][:], 0.0), writes=[T["rb2"]])
        for s in range(nseq):
            E("pool", lambda e, s=s: e.memset(T[f"nch{s}"][:], 0.0), writes=[T[f"nch{s}"]])
            E("pool", lambda e, s=s: e.memset(T[f"ncsf{s}"][:], 0.0), writes=[T[f"ncsf{s}"]])
            E("pool", lambda e, s=s: e.memset(T[f"zlast{s}"][:], 0.0), writes=[T[f"zlast{s}"]])

        tps = ["t0", "t1"]

        def emit_prep():
            widx = 0
            stgs = ["bsc", "xb2"]

            def prep_w(dw, dname, douts):
                nonlocal widx
                step = 1024 // douts
                for k0 in range(0, KCH, step):
                    stg = T[stgs[widx % 2]]
                    E(f"dma:wf{widx % 2}@act",
                      lambda e, dw=dw, k0=k0, stg=stg, step=step, douts=douts:
                      e.dma_start(
                          stg[:].rearrange("p (c j) -> p c j", c=step),
                          dw.rearrange("(c p) j -> p c j", p=128)[:, k0:k0 + step, :]
                          if step > 1 else dw[k0 * 128:(k0 + 1) * 128, :].unsqueeze(1)),
                      writes=[stg])
                    co = k0 * douts
                    E("act", lambda e, stg=stg, co=co, dname=dname:
                      e.activation(out=T[dname + "_h"][:, co:co + 1024], in_=stg[:],
                                   func=AF.Copy),
                      reads=[stg], writes=[T[dname + "_h"]])
                    widx += 1

            def prep_w16(dw, dname):
                nonlocal widx
                stg = T[stgs[widx % 2]]
                E(f"dma:wf{widx % 2}@act",
                  lambda e, dw=dw, stg=stg: e.dma_start(
                      stg[:].rearrange("p (c j) -> p c j", c=8)[:, :, 0:16],
                      dw.rearrange("(c p) j -> p c j", p=128)),
                  writes=[stg])
                E("act", lambda e, stg=stg, dname=dname:
                  e.activation(out=T[dname + "_h"][:],
                               in_=stg[:].rearrange("p (c j) -> p c j", c=8)[:, :, 0:16],
                               func=AF.Copy),
                  reads=[stg], writes=[T[dname + "_h"]])
                widx += 1

            prep_w(dwq, "wq", 512)
            prep_w(dwk, "wk", 512)
            prep_w(dwal, "wal", 512)
            prep_w(dwv, "wv", 1024)
            prep_w(dwb, "wb", 1024)
            prep_w(dwo, "wo", 1024)
            prep_w16(drml, "rm")
            E("dma:wf0@act", lambda e: e.dma_start(T["bsc"][0:64, :], dlb), writes=[T["bsc"]])
            E("act", lambda e: e.activation(out=T["lb_h"][0:64, :], in_=T["bsc"][0:64, :],
                                            func=AF.Copy),
              reads=[T["bsc"]], writes=[(T["lb_h"], "a")])
            E("dve", lambda e: e.tensor_tensor(T["lb_l"][0:64, :], T["bsc"][0:64, :],
                                               T["lb_h"][0:64, :], op=ALU.subtract),
              reads=[T["bsc"], (T["lb_h"], "a")], writes=[(T["lb_l"], "a")])
            E("dma:wf1@act", lambda e: e.dma_start(T["bsc"][64:65, :], dbb), writes=[T["bsc"]])
            E("dve", lambda e: e.tensor_copy(T["lb_h"][64:65, :], T["bsc"][64:65, :]),
              reads=[T["bsc"]], writes=[(T["lb_h"], "b")])
            E("dve", lambda e: e.tensor_tensor(T["lb_l"][64:65, :], T["bsc"][64:65, :],
                                               T["lb_h"][64:65, :], op=ALU.subtract),
              reads=[T["bsc"], (T["lb_h"], "b")], writes=[(T["lb_l"], "b")])
            E("dma:wf1@act", lambda e: e.dma_start(T["bsc"][0:1, 0:512], dwab), writes=[T["bsc"]])
            E("dve", lambda e: e.tensor_copy(T["vh_h"][0:1, 0:512], T["bsc"][0:1, 0:512]),
              reads=[T["bsc"]], writes=[T["vh_h"]])
            E("dve", lambda e: e.tensor_tensor(T["yt_h"][0:1, 0:512], T["bsc"][0:1, 0:512],
                                               T["vh_h"][0:1, 0:512], op=ALU.subtract),
              reads=[T["bsc"], T["vh_h"]], writes=[T["yt_h"]])
            E("dma:wr@act", lambda e: e.dma_start(T["wab2"][0:1, :], T["vh_h"][0:1, 0:512]),
              reads=[T["vh_h"]], writes=[T["wab2"]])
            E("dma:wr@act", lambda e: e.dma_start(T["wab2"][1:2, :], T["yt_h"][0:1, 0:512]),
              reads=[T["yt_h"]], writes=[T["wab2"]])
            E("dma:wf1@act", lambda e: e.dma_start(T["bsc"][0:1, 0:8], drb), writes=[T["bsc"]])
            E("dve", lambda e: e.tensor_copy(T["vh_h"][0:1, 0:8], T["bsc"][0:1, 0:8]),
              reads=[T["bsc"]], writes=[T["vh_h"]])
            E("dve", lambda e: e.tensor_tensor(T["yt_h"][0:1, 0:8], T["bsc"][0:1, 0:8],
                                               T["vh_h"][0:1, 0:8], op=ALU.subtract),
              reads=[T["bsc"], T["vh_h"]], writes=[T["yt_h"]])
            E("dma:wr@act", lambda e: e.dma_start(T["rb2"][0:1, 0:8], T["vh_h"][0:1, 0:8]),
              reads=[T["vh_h"]], writes=[T["rb2"]])
            E("dma:wr@act", lambda e: e.dma_start(T["rb2"][1:2, 0:8], T["yt_h"][0:1, 0:8]),
              reads=[T["yt_h"]], writes=[T["rb2"]])

        slots = [(s, ch) for ch in range(NCH) for s in range(nseq)]
        NS = len(slots)

        def load_slot(si):
            if si >= NS:
                return
            s, ch = slots[si]
            xt = T[f"xb{si % 3}"]
            E(f"dma:x{si % 3}@sp",
              lambda e, s=s, ch=ch, xt=xt: e.dma_start(xt[:], dx[s, ch * TC:(ch + 1) * TC, :]),
              writes=[xt])

        def xnth(k):
            return T["xnt_h"][:, k * 128:(k + 1) * 128]

        def proj_group(dest_ap, dest_res, wname, co_fn, nblk, extra=None,
                       extra_reads=()):
            wh = T[wname + "_h"]

            def g(e):
                last = None
                for k in range(KCH):
                    co = co_fn(k)
                    last = e.matmul(dest_ap, xnth(k), wh[:, co:co + nblk],
                                    start=(k == 0),
                                    stop=(k == KCH - 1 and extra is None))
                if extra is not None:
                    last = extra(e)
                return last
            reads = [T["xnt_h"], wh] + list(extra_reads)
            E("pe", g, reads=reads, writes=[dest_res])

        import os as _os
        TRB = _os.environ.get("KERN_TRBATCH", "1") == "1"
        PIPE = _os.environ.get("KERN_PIPE", "1") == "1"
        SBANKS = ("pal", "pw")
        WMASK_MM = _os.environ.get("KERN_WMASK", "dve") == "mm"

        def tr_batch(src, src_res, dh, dh_res, base):
            if not TRB:
                for k in range(KCH):
                    tp = T[tps[k % 2]]
                    E("pe", lambda e, k=k, tp=tp: e.transpose(
                        tp[:, 0:128], src[:, k * 128:(k + 1) * 128], T["idf"][:]),
                      reads=[src_res, T["idf"]], writes=[tp])
                    co = base + k * 128
                    E("act", lambda e, tp=tp, co=co: e.activation(
                        out=dh[:, co:co + 128], in_=tp[:, 0:128], func=AF.Copy),
                      reads=[tp], writes=[dh_res])
                return
            for half in range(2):
                tp = T[tps[half]]

                def tr4(e, half=half, tp=tp):
                    last = None
                    for j in range(4):
                        k = half * 4 + j
                        last = e.transpose(tp[:, j * 128:(j + 1) * 128],
                                           src[:, k * 128:(k + 1) * 128], T["idf"][:])
                    return last
                E("pe", tr4, reads=[src_res, T["idf"]], writes=[tp])
                co = base + half * 512
                E("act", lambda e, tp=tp, co=co: e.activation(
                    out=dh[:, co:co + 512], in_=tp[:], func=AF.Copy),
                  reads=[tp], writes=[(dh_res, f"h{half}")])

        def front(si):
            if si >= NS:
                return
            s, ch = slots[si]
            xt = T[f"xb{si % 3}"]
            load_slot(si + 1)
            E("dve", lambda e, xt=xt: e.tensor_reduce(T["xsum"][:], xt[:], axis=AX.X, op=ALU.add),
              reads=[xt], writes=[T["xsum"]])
            E("act", lambda e, xt=xt: e.activation(out=T["xnf"][:], in_=xt[:], func=AF.Square,
                                                   accum_out=T["ssq"][:]),
              reads=[xt], writes=[T["xnf"], T["ssq"]])
            E("dve", lambda e: e.tensor_scalar_mul(T["mean"][:], T["xsum"][:], 1.0 / D),
              reads=[T["xsum"]], writes=[T["mean"]])
            E("dve", lambda e: e.tensor_tensor(T["m2"][:], T["mean"][:], T["mean"][:], op=ALU.mult),
              reads=[T["mean"]], writes=[T["m2"]])
            E("dve", lambda e: e.scalar_tensor_tensor(T["var"][:], T["ssq"][:], 1.0 / D,
                                                      T["m2"][:], op0=ALU.mult, op1=ALU.subtract),
              reads=[T["ssq"], T["m2"]], writes=[T["var"]])
            E("dve", lambda e: e.tensor_scalar_add(T["var"][:], T["var"][:], 1e-5),
              reads=[T["var"]], writes=[T["var"]])
            E("act", lambda e: e.activation(out=T["rstd"][:], in_=T["var"][:], func=AF.Ln),
              reads=[T["var"]], writes=[T["rstd"]])
            E("act", lambda e: e.activation(out=T["rstd"][:], in_=T["rstd"][:], func=AF.Exp, scale=-0.5),
              reads=[T["rstd"]], writes=[T["rstd"]])
            E("dve", lambda e: e.tensor_tensor(T["mrstd"][:], T["mean"][:], T["rstd"][:], op=ALU.mult),
              reads=[T["mean"], T["rstd"]], writes=[T["mrstd"]])
            E("dve", lambda e, xt=xt: e.tensor_scalar(T["xnf"][:], xt[:], T["rstd"][:], T["mrstd"][:],
                                                      op0=ALU.mult, op1=ALU.subtract),
              reads=[xt, T["rstd"], T["mrstd"]], writes=[T["xnf"]])
            tr_batch(T["xnf"], T["xnf"], T["xnt_h"], T["xnt_h"], 0)

        def mid(si):
            s, ch = slots[si]
            xt = T[f"xb{si % 3}"]
            nch_, ncf = T[f"nch{s}"], T[f"ncsf{s}"]
            zlast = T[f"zlast{s}"]

            for part in range(2):
                def qmm(e, part=part):
                    last = None
                    for k in range(part * 4, part * 4 + 4):
                        co = k * 512
                        last = e.matmul(T["pq"][:], xnth(k), T["wq_h"][:, co:co + 512],
                                        start=(k == 0), stop=(k == KCH - 1))
                    return last
                E("pe", qmm,
                  reads=[(T["xnt_h"], f"h{part}"), T["wq_h"]],
                  writes=[T["pq"]])
            proj_group(T["pk"][:], T["pk"], "wk", lambda k: k * 512, 512)
            proj_group(T["t0"][:, 0:16], T["t0"], "rm", lambda k: k * 16, 16,
                       extra=lambda e: e.matmul(T["t0"][:, 0:16], T["ones2"][:], T["rb2"][:],
                                                start=False, stop=True),
                       extra_reads=[T["ones2"], T["rb2"]])
            proj_group(T["pal"][:], T["pal"], "wal", lambda k: k * 512, 512,
                       extra=lambda e: e.matmul(T["pal"][:], T["ones2"][:], T["wab2"][:],
                                                start=False, stop=True),
                       extra_reads=[T["ones2"], T["wab2"]])

            E("dve", lambda e: e.tensor_copy(T["xnf"][:, 0:512], T["pq"][:]),
              reads=[T["pq"]], writes=[(T["xnf"], "sa")])
            E("dve", lambda e: e.tensor_tensor(T["xnf"][:, 512:1024], T["xnf"][:, 0:512],
                                               T["pk"][:], op=ALU.mult),
              reads=[(T["xnf"], "sa"), T["pk"]], writes=[(T["xnf"], "sb")])
            E("dve", lambda e: e.tensor_reduce(
                T["bq"][:].unsqueeze(2),
                T["xnf"][:, 512:1024].rearrange("p (h w) -> p h w", h=H),
                axis=AX.X, op=ALU.add),
              reads=[(T["xnf"], "sb")], writes=[T["bq"]])

            E("dve", lambda e: e.tensor_reduce(T["nmx"][:], T["t0"][:, 0:8], axis=AX.X, op=ALU.max,
                                               negate=True),
              reads=[T["t0"]], writes=[T["nmx"]])
            E("act", lambda e: e.activation(out=T["gf"][:], in_=T["t0"][:, 0:8], func=AF.Exp,
                                            bias=T["nmx"][:]),
              reads=[T["t0"], T["nmx"]], writes=[T["gf"]])
            E("dve", lambda e: e.tensor_reduce(T["sumg"][:], T["gf"][:], axis=AX.X, op=ALU.add),
              reads=[T["gf"]], writes=[T["sumg"]])
            E("dve", lambda e: e.reciprocal(T["invg"][:], T["sumg"][:]),
              reads=[T["sumg"]], writes=[T["invg"]])
            E("dve", lambda e: e.tensor_scalar(T["gf"][:], T["gf"][:], T["invg"][:], None,
                                               op0=ALU.mult),
              reads=[T["gf"], T["invg"]], writes=[T["gf"]])
            E("dve", lambda e: e.tensor_copy(T["gbits"][:].bitcast(BF16), T["gf"][:]),
              reads=[T["gf"]], writes=[T["gbits"]])
            E("dve", lambda e: e.tensor_copy(T["grf"][:], T["gbits"][:].bitcast(BF16)),
              reads=[T["gbits"]], writes=[T["grf"]])
            E("dve", lambda e: e.tensor_scalar_mul(T["midf"][:], T["t0"][:, 8:16], 2.0),
              reads=[T["t0"]], writes=[T["midf"]])

            proj_group(T["pv"][:, 0:512], (T["pv"], "h0"), "wv", lambda k: k * 1024, 512)
            proj_group(T["pv"][:, 512:1024], (T["pv"], "h1"), "wv", lambda k: k * 1024 + 512, 512)

            E("act", lambda e: e.activation(out=T["xnf"][:, 0:512], in_=T["pal"][:], func=AF.Exp),
              reads=[T["pal"]], writes=[(T["xnf"], "sa")])
            E("act", lambda e: e.activation(out=T["xnf"][:, 0:512], in_=T["xnf"][:, 0:512],
                                            func=AF.Ln, bias=1.0),
              reads=[(T["xnf"], "sa")], writes=[(T["xnf"], "sa")])
            E("dve", lambda e: e.tensor_reduce(
                T["av"][:].unsqueeze(2),
                T["xnf"][:, 0:512].rearrange("p (h w) -> p h w", h=H),
                axis=AX.X, op=ALU.add, negate=True),
              reads=[(T["xnf"], "sa")], writes=[T["av"]])

            E("act", lambda e: e.activation(out=T["vh_h"][:, 0:512], in_=T["pv"][:, 0:512],
                                            func=AF.Copy),
              reads=[(T["pv"], "h0")], writes=[(T["vh_h"], "h0")])
            E("dve", lambda e: e.tensor_copy(T["vh_h"][:, 512:1024], T["pv"][:, 512:1024]),
              reads=[(T["pv"], "h1")], writes=[(T["vh_h"], "h1")])
            E("pe", lambda e: e.transpose(T["t1"][0:8, 0:128], T["av"][:], T["idf"][:]),
              reads=[T["av"], T["idf"]], writes=[T["t1"]])
            E("dve", lambda e: e.tensor_copy(T["atf"][:], T["t1"][0:8, 0:128]),
              reads=[T["t1"]], writes=[T["atf"]])
            E("act", lambda e: e.activation(out=T["eat"][:], in_=T["t1"][0:8, 0:128], func=AF.Exp),
              reads=[T["t1"]], writes=[T["eat"]])
            E("pe", lambda e: e.transpose(T["t0"][0:8, 0:128], T["bq"][:], T["idf"][:]),
              reads=[T["bq"], T["idf"]], writes=[T["t0"]])
            E("act", lambda e: e.activation(out=T["ebt"][:], in_=T["t0"][0:8, 0:128], func=AF.Exp,
                                            scale=0.125),
              reads=[T["t0"]], writes=[T["ebt"]])
            E("pe", lambda e: e.transpose(T["t1"][0:8, 128:256], T["grf"][:], T["idf"][:]),
              reads=[T["grf"], T["idf"]], writes=[T["t1"]])
            E("dve", lambda e: e.tensor_copy(T["gtf"][:], T["t1"][0:8, 128:256]),
              reads=[T["t1"]], writes=[T["gtf"]])
            E("pe", lambda e: e.transpose(T["t0"][0:8, 128:256], T["midf"][:], T["idf"][:]),
              reads=[T["midf"], T["idf"]], writes=[T["t0"]])
            E("dve", lambda e: e.tensor_copy(T["mtf"][:], T["t0"][0:8, 128:256]),
              reads=[T["t0"]], writes=[T["mtf"]])
            E("pe", lambda e: e.matmul(T["t1"][0:64, 256:384], T["repe"][:], T["gtf"][:],
                                       start=True, stop=True),
              reads=[T["repe"], T["gtf"]], writes=[T["t1"]])
            E("dve", lambda e: e.tensor_copy(T["m1f"][:], T["t1"][0:64, 256:384]),
              reads=[T["t1"]], writes=[T["m1f"]])
            E("pe", lambda e: e.matmul(T["t0"][0:64, 256:384], T["repr"][:], T["mtf"][:],
                                       start=True, stop=True),
              reads=[T["repr"], T["mtf"]], writes=[T["t0"]])
            E("dve", lambda e: e.tensor_tensor(T["gff"][:], T["m1f"][:], T["t0"][0:64, 256:384],
                                               op=ALU.mult),
              reads=[T["m1f"], T["t0"]], writes=[T["gff"]])
            E("dve", lambda e: e.tensor_copy(T["g65_h"][0:64, :], T["gff"][:]),
              reads=[T["gff"]], writes=[(T["g65_h"], "a")])

            for hf in range(2):
                def umm(e, hf=hf):
                    dst = T["pv"][:, hf * 512:(hf + 1) * 512]
                    for k in range(KCH):
                        co = k * 1024 + hf * 512
                        e.matmul(dst, xnth(k), T["wb_h"][:, co:co + 512],
                                 start=(k == 0), stop=False)
                    sl = slice(hf * 512, (hf + 1) * 512)
                    e.matmul(dst, T["g65_h"][:], T["lb_h"][:, sl], start=False, stop=False)
                    return e.matmul(dst, T["g65_h"][:], T["lb_l"][:, sl], start=False, stop=True)
                E("pe", umm,
                  reads=[T["xnt_h"], T["wb_h"],
                         T["g65_h"], T["lb_h"], T["lb_l"]],
                  writes=[(T["pv"], f"h{hf}")])

            E("dve", lambda e: e.tensor_tensor_scan(T["ptt"][:], T["atf"][:], T["atf"][:], 0.0,
                                                    op0=ALU.add, op1=ALU.bypass),
              reads=[T["atf"]], writes=[T["ptt"]])
            E("dve", lambda e, zlast=zlast: e.tensor_tensor_scan(
                T["zt"][:], T["eat"][:], T["ebt"][:], zlast[:], op0=ALU.mult, op1=ALU.add),
              reads=[T["eat"], T["ebt"], zlast], writes=[T["zt"]])
            E("dve", lambda e, zlast=zlast: e.tensor_copy(zlast[:], T["zt"][:, 127:128]),
              reads=[T["zt"]], writes=[zlast])
            E("act", lambda e: e.activation(out=T["lnzt"][:], in_=T["zt"][:], func=AF.Ln),
              reads=[T["zt"]], writes=[T["lnzt"]])
            E("pool", lambda e: e.tensor_tensor(T["pzt"][:], T["ptt"][:], T["lnzt"][:],
                                                op=ALU.subtract),
              reads=[T["ptt"], T["lnzt"]], writes=[T["pzt"]])
            E("act", lambda e: e.activation(out=T["cwt"][:], in_=T["pzt"][:], func=AF.Exp),
              reads=[T["pzt"]], writes=[T["cwt"]])
            E("pool", lambda e: e.tensor_copy(T["cwh"][:], T["cwt"][:]),
              reads=[T["cwt"]], writes=[T["cwh"]])
            E("dve", lambda e: e.scalar_tensor_tensor(T["bmpt"][:], T["t0"][0:8, 0:128], 0.125,
                                                      T["ptt"][:], op0=ALU.mult,
                                                      op1=ALU.subtract),
              reads=[T["t0"], T["ptt"]], writes=[T["bmpt"]])
            E("act", lambda e: e.activation(out=T["eet"][:], in_=T["bmpt"][:], func=AF.Exp,
                                            bias=T["ptt"][:, 127:128]),
              reads=[T["bmpt"], T["ptt"]], writes=[T["eet"]])
            E("act", lambda e: e.activation(out=T["epend"][:], in_=T["ptt"][:, 127:128],
                                            func=AF.Exp),
              reads=[T["ptt"]], writes=[T["epend"]])
            E("pe", lambda e: e.transpose(T["t0"][0:128, 0:8], T["eet"][:], T["id8"][:]),
              reads=[T["eet"], T["id8"]], writes=[T["t0"]])
            E("act", lambda e: e.activation(out=T["eecol_h"][:], in_=T["t0"][:, 0:8], func=AF.Copy),
              reads=[T["t0"]], writes=[T["eecol_h"]])

            for hb in range(2):
                Sb = T["pq"] if hb == 0 else T["pk"]

                def selmm(e, Sb=Sb, hb=hb):
                    last = None
                    for i in range(4):
                        h = hb * 4 + i
                        e.matmul(Sb[:, i * 128:(i + 1) * 128],
                                 T["id8"][:, h:h + 1].broadcast_to([8, 128]),
                                 T["pzt"][:], start=True, stop=False)
                        last = e.matmul(Sb[:, i * 128:(i + 1) * 128],
                                        T["bmpt"][:],
                                        T["id8"][:, h:h + 1].broadcast_to([8, 128]),
                                        start=False, stop=True)
                    return last
                E("pe", selmm, reads=[T["id8"], T["pzt"], T["bmpt"]], writes=[Sb])
            for hb in range(2):
                Sb = T["pq"] if hb == 0 else T["pk"]
                E("dve", lambda e, Sb=Sb, hb=hb: e.tensor_tensor(
                    T["bsc"][:, hb * 512:(hb + 1) * 512], Sb[:], T["m0r"][:], op=ALU.add),
                  reads=[Sb, T["m0r"]], writes=[T["bsc"]])
            E("act", lambda e: e.activation(out=T["bsc"][:], in_=T["bsc"][:], func=AF.Exp),
              reads=[T["bsc"]], writes=[T["bsc"]])
            E("pool", lambda e: e.tensor_tensor(T["bsc"][:, 0:512], T["bsc"][:, 0:512],
                                                T["bsc"][:, 512:1024], op=ALU.add),
              reads=[T["bsc"]], writes=[T["bsc"]])
            E("dve", lambda e: e.tensor_reduce(
                T["wstf"][:].unsqueeze(2),
                T["bsc"][:, 0:512].rearrange("p (a t) -> p t a", a=4),
                axis=AX.X, op=ALU.add),
              reads=[T["bsc"]], writes=[T["wstf"]])
            E("act", lambda e: e.activation(out=T["wst_h"][:], in_=T["wstf"][:], func=AF.Copy),
              reads=[T["wstf"]], writes=[T["wst_h"]])

            for hf, bank in [(0, "t0"), (1, "t1")]:
                def nemm(e, hf=hf, bank=bank):
                    dst = T[bank][0:8, 0:512]
                    sl = slice(hf * 512, (hf + 1) * 512)
                    return e.matmul(dst, T["eecol_h"][:], T["vh_h"][:, sl],
                                    start=True, stop=True)
                E("pe", nemm,
                  reads=[T["eecol_h"], T["vh_h"]],
                  writes=[T[bank]])

            for bi, bank in [(0, "pal"), (1, "pw")]:
                def ycar(e, bi=bi, bank=bank):
                    last = None
                    for i in range(4):
                        db = bi * 4 + i
                        dst = T[bank][:, i * 128:(i + 1) * 128]
                        sl = slice(db * 128, (db + 1) * 128)
                        last = e.matmul(dst, nch_[0:8, sl], T["cwh"][:],
                                        start=(i == 0), stop=False)
                    return last
                E("pe", ycar,
                  reads=[T["cwh"], nch_], writes=[T[bank]])
            for bi, bank in [(0, "pal"), (1, "pw")]:
                def yws(e, bi=bi, bank=bank):
                    last = None
                    for i in range(4):
                        db = bi * 4 + i
                        dst = T[bank][:, i * 128:(i + 1) * 128]
                        sl = slice(db * 128, (db + 1) * 128)
                        last = e.matmul(dst, T["vh_h"][:, sl], T["wst_h"][:], start=False,
                                        stop=(i == 3))
                    return last
                E("pe", yws,
                  reads=[T["wst_h"], T["vh_h"]], writes=[T[bank]])

            E("act", lambda e: e.activation(
                out=T["yt_h"][:, 0:512], in_=T["pal"][:], func=AF.Copy),
              reads=[T["pal"]], writes=[(T["yt_h"], "h0")])
            E("dve", lambda e: e.tensor_copy(T["yt_h"][:, 512:1024], T["pw"][:]),
              reads=[T["pw"]], writes=[(T["yt_h"], "h1")])

            for hf, bank in [(0, SBANKS[0]), (1, SBANKS[1])]:
                for part in range(2):
                    def smm(e, hf=hf, bank=bank, part=part):
                        dst = T[bank][:]
                        last = None
                        for k in range(part * 4, part * 4 + 4):
                            co = k * 1024 + hf * 512
                            last = e.matmul(dst, T["yt_h"][:, k * 128:(k + 1) * 128],
                                            T["wo_h"][:, co:co + 512],
                                            start=(k == 0), stop=(k == KCH - 1))
                        return last
                    E("pe", smm,
                      reads=[(T["yt_h"], f"h{part}"), T["wo_h"]],
                      writes=[T[bank]])

            for hf, bank in [(0, "t0"), (1, "t1")]:
                E("dve", lambda e, hf=hf, bank=bank, ncf=ncf: e.scalar_tensor_tensor(
                    ncf[:, hf * 512:(hf + 1) * 512], ncf[:, hf * 512:(hf + 1) * 512],
                    T["epend"][:], T[bank][0:8, 0:512], op0=ALU.mult, op1=ALU.add),
                  reads=[(ncf, f"h{hf}"), T["epend"], T[bank]], writes=[(ncf, f"h{hf}")])
            E("pool", lambda e, nch_=nch_, ncf=ncf: e.tensor_copy(nch_[:], ncf[:]),
              reads=[(ncf, "h0"), (ncf, "h1")], writes=[nch_])


        def back(si):
            s, ch = slots[si]
            xt = T[f"xb{si % 3}"]
            E("act", lambda e: e.activation(out=T["bsc"][:], in_=T["pv"][:], func=AF.Silu),
              reads=[(T["pv"], "h0"), (T["pv"], "h1")], writes=[T["bsc"]])
            E("dve", lambda e: e.tensor_tensor(T["bsc"][:, 0:512], T["bsc"][:, 0:512],
                                               T[SBANKS[0]][:], op=ALU.mult),
              reads=[T["bsc"], T[SBANKS[0]]], writes=[T["bsc"]])
            E("dve", lambda e: e.tensor_tensor(T["bsc"][:, 512:1024], T["bsc"][:, 512:1024],
                                               T[SBANKS[1]][:], op=ALU.mult),
              reads=[T["bsc"], T[SBANKS[1]]], writes=[T["bsc"]])
            E("pool", lambda e, xt=xt: e.tensor_tensor(xt[:], T["bsc"][:], xt[:], op=ALU.add),
              reads=[T["bsc"], xt], writes=[xt])
            E(f"dma:o{si % 3}@sp",
              lambda e, s=s, ch=ch, xt=xt: e.dma_start(dout[s, ch * TC:(ch + 1) * TC, :], xt[:]),
              reads=[xt])

        load_slot(0)
        front(0)
        emit_prep()
        for si in range(NS):
            mid(si)
            front(si + 1)
            back(si)

        pg.finalize(es, end_waits=pg.barrier_token())
    n_instr = sum(len(v) for v in pg.ops.values())
    return nc, {"n_instr": n_instr, "n_waits": pg.n_waits, "n_bumps": pg.n_bumps}


def stage_inputs(inputs, core, n_cores=8):
    x = inputs["L0_attn_x"]
    B = x.shape[0]
    per = B // n_cores
    return {
        "x": np.ascontiguousarray(x[core * per:(core + 1) * per]),
        "wq": np.ascontiguousarray(inputs["L0_attn_W_q"].T),
        "wk": np.ascontiguousarray(inputs["L0_attn_W_k"].T),
        "wal": np.ascontiguousarray(inputs["L0_attn_W_alpha_w"].T),
        "wv": np.ascontiguousarray(inputs["L0_attn_W_v"].T),
        "wo": np.ascontiguousarray(inputs["L0_attn_W_o"].T),
        "wb": np.ascontiguousarray(inputs["L0_ffn_base_w"].T),
        "wab": np.ascontiguousarray(inputs["L0_attn_W_alpha_b"].reshape(1, 512)),
        "bb": np.ascontiguousarray(inputs["L0_ffn_base_b"].reshape(1, D)),
        "rml": np.ascontiguousarray(
            np.concatenate([inputs["L0_ffn_router_w"].T,
                            inputs["L0_ffn_lora_A"]], axis=1)),
        "lb": np.ascontiguousarray(inputs["L0_ffn_lora_B"].reshape(64, D)),
        "rb": np.ascontiguousarray(inputs["L0_ffn_router_b"].reshape(1, 8)),
    }


_NC_CACHE = {}


def kernel(**inputs):
    from concourse.bass_utils import run_bass_kernel_spmd
    n_cores = 8
    if "nc" not in _NC_CACHE:
        _NC_CACHE["nc"] = build_nc(S=2048, nseq=2)[0]
    nc = _NC_CACHE["nc"]
    in_maps = [stage_inputs(inputs, c, n_cores=n_cores) for c in range(n_cores)]
    res = run_bass_kernel_spmd(nc, in_maps, core_ids=list(range(n_cores)))
    out = np.concatenate([r["out"] for r in res.results], axis=0)
    return np.ascontiguousarray(out.reshape(16, 2048, 1024), dtype=np.float32)


# version 21
# speedup vs baseline: 1.3763x; 1.1195x over previous
import contextlib
import numpy as np
import concourse.bass as bass
import concourse.mybir as mybir

F32 = mybir.dt.float32
BF16 = mybir.dt.bfloat16
I16 = mybir.dt.int16
AF = mybir.ActivationFunctionType
ALU = mybir.AluOpType
AX = mybir.AxisListType

D = 1024
H = 8
KCH = 8
TC = 128
EPS = 1e-8

ENGINES = ("pe", "act", "dve", "pool", "sp")


def _name(x):
    if isinstance(x, tuple):
        h, r = x
    else:
        h, r = x, None
    nm = getattr(h, "name", None)
    if nm is None:
        t = getattr(h, "tensor", None)
        nm = getattr(t, "name", None) if t is not None else str(h)
    return nm, r


class Prog:
    def __init__(self, nc):
        self.nc = nc
        self.counts = {e: 0 for e in ENGINES}
        self.ops = {e: [] for e in ENGINES}
        self.last_write = {}
        self.readers = {}
        self.waited = {e: {} for e in ENGINES}
        self.n_waits = 0
        self.n_bumps = 0
        self.dma_streams = set()
        self.psum_names = set()

    def _ensure_stream(self, s):
        if s not in self.counts:
            self.counts[s] = 0
            self.waited[s] = {}
            self.dma_streams.add(s)

    def _tokens_for_read(self, key):
        nm, r = key
        d = self.last_write.get(nm)
        if not d:
            return []
        if r is None:
            return list(d.values())
        out = []
        if r in d:
            out.append(d[r])
        if None in d:
            out.append(d[None])
        return out

    def _tokens_for_write(self, key):
        toks = list(self._tokens_for_read(key))
        nm, r = key
        d = self.readers.get(nm)
        if d:
            if r is None:
                for lst in d.values():
                    toks.extend(lst)
            else:
                toks.extend(d.get(r, []))
                toks.extend(d.get(None, []))
        return toks

    def emit(self, eng, fn, reads=(), writes=(), tag=None):
        if eng.startswith("dma:"):
            stream, queue = eng[4:].split("@")
            stream = "dma_" + stream
            self._ensure_stream(stream)
            issue_eng, cnt_eng, inc = queue, stream, 16
        else:
            issue_eng, cnt_eng, inc = eng, eng, 1

        rkeys = [_name(x) for x in reads]
        wkeys = [_name(x) for x in writes]
        pr = [k for k in rkeys if k[0] in self.psum_names]
        if pr:
            rkeys = [k for k in rkeys if k[0] not in self.psum_names]
            wkeys = wkeys + pr
        need = {}

        def add_need(peng, pcnt):
            if peng == cnt_eng and cnt_eng in self.dma_streams:
                return
            if peng in self.dma_streams and pcnt < self.counts[peng]:
                pcnt = self.counts[peng]
                self.n_bumps += 1
            if self.waited[issue_eng].get(peng, 0) >= pcnt:
                return
            need[peng] = max(need.get(peng, 0), pcnt)

        if cnt_eng in self.dma_streams and self.counts[cnt_eng] > 0:
            if self.waited[issue_eng].get(cnt_eng, 0) < self.counts[cnt_eng]:
                need[cnt_eng] = self.counts[cnt_eng]
        for k in rkeys:
            for peng, pcnt in self._tokens_for_read(k):
                add_need(peng, pcnt)
        for k in wkeys:
            for peng, pcnt in self._tokens_for_write(k):
                add_need(peng, pcnt)
        for peng, pcnt in need.items():
            self.waited[issue_eng][peng] = max(self.waited[issue_eng].get(peng, 0), pcnt)
        self.n_waits += len(need)

        self.counts[cnt_eng] += inc
        tok = (cnt_eng, self.counts[cnt_eng])
        for nm, r in wkeys:
            d = self.last_write.setdefault(nm, {})
            rd = self.readers.setdefault(nm, {})
            if r is None:
                d.clear()
                rd.clear()
                d[None] = tok
            else:
                d[r] = tok
                rd.pop(r, None)
        for nm, r in rkeys:
            self.readers.setdefault(nm, {}).setdefault(r, []).append(tok)

        self.ops[issue_eng].append((dict(need), fn, cnt_eng, inc, tag))

    def barrier_token(self):
        return dict(self.counts)

    def finalize(self, es, end_waits=None):
        nc = self.nc
        sems = {}
        for e in self.counts:
            if self.counts[e] > 0 or e == "sp":
                sems[e] = es.enter_context(nc.semaphore(f"sem_{e}"))
        self.sems = sems
        block = es.enter_context(nc.Block())
        eng_attr = {"pe": "tensor", "act": "scalar", "dve": "vector",
                    "pool": "gpsimd", "sp": "sync"}

        def make_body(ename):
            def body(engine):
                for need, fn, cnt_eng, inc, tag in self.ops[ename]:
                    for peng, pcnt in sorted(need.items()):
                        engine.wait_ge(sems[peng], pcnt)
                    instr = fn(engine)
                    if instr is not None:
                        instr.then_inc(sems[cnt_eng], inc)
                if ename == "sp" and end_waits:
                    for peng, pcnt in sorted(end_waits.items()):
                        if pcnt > 0 and peng in sems:
                            engine.wait_ge(sems[peng], pcnt)
            return body

        for ename in ENGINES:
            if self.ops[ename] or (ename == "sp" and end_waits):
                getattr(block, eng_attr[ename])(make_body(ename))


def build_nc(S=2048, nseq=2):
    NCH = S // TC
    nc = bass.Bass("TRN2", target_bir_lowering=False, debug=False)

    dx = nc.dram_tensor("x", [nseq, S, D], F32, kind="ExternalInput").ap()
    dwq = nc.dram_tensor("wq", [D, 512], F32, kind="ExternalInput").ap()
    dwk = nc.dram_tensor("wk", [D, 512], F32, kind="ExternalInput").ap()
    dwal = nc.dram_tensor("wal", [D, 512], F32, kind="ExternalInput").ap()
    dwv = nc.dram_tensor("wv", [D, D], F32, kind="ExternalInput").ap()
    dwo = nc.dram_tensor("wo", [D, D], F32, kind="ExternalInput").ap()
    dwb = nc.dram_tensor("wb", [D, D], F32, kind="ExternalInput").ap()
    dwab = nc.dram_tensor("wab", [1, 512], F32, kind="ExternalInput").ap()
    dbb = nc.dram_tensor("bb", [1, D], F32, kind="ExternalInput").ap()
    drml = nc.dram_tensor("rml", [D, 16], F32, kind="ExternalInput").ap()
    dlb = nc.dram_tensor("lb", [64, D], F32, kind="ExternalInput").ap()
    drb = nc.dram_tensor("rb", [1, 8], F32, kind="ExternalInput").ap()
    dout = nc.dram_tensor("out", [nseq, S, D], F32, kind="ExternalOutput").ap()
    dpzs = nc.dram_tensor("pzs", [8, 128], F32, kind="Internal").ap()

    es = contextlib.ExitStack()
    pg = Prog(nc)
    with es:
        T = {}

        def sb(name, shp, dt=F32):
            T[name] = es.enter_context(nc.sbuf_tensor(name, shp, dt))
            return T[name]

        def ps(name, shp, dt=F32):
            T[name] = es.enter_context(nc.psum_tensor(name, shp, dt))
            pg.psum_names.add(name)
            return T[name]

        for w, cols in [("wq", 8 * 512), ("wk", 8 * 512), ("wal", 8 * 512),
                        ("wv", 8 * 1024), ("wo", 8 * 1024), ("wb", 8 * 1024)]:
            sb(w + "_h", [128, cols], BF16)
        sb("rm_h", [128, 128], BF16)
        sb("lb_h", [65, D], BF16)
        sb("lb_l", [65, D], BF16)
        sb("wab2", [2, 512], BF16)
        sb("rb2", [2, 16], BF16)
        sb("ones2", [2, 128], BF16)
        sb("idf", [128, 128], F32)
        sb("m0", [128, 128], F32)
        sb("m0r", [128, 512], F32)
        sb("id8", [8, 8], F32)
        sb("repe", [8, 64], F32)
        sb("repr", [8, 64], F32)
        for i in range(3):
            sb(f"xb{i}", [128, D], F32)
        sb("xnt_h", [128, D], BF16)
        sb("vh_h", [128, D], BF16)
        sb("yt_h", [128, D], BF16)
        for s in range(nseq):
            sb(f"nch{s}", [8, D], BF16)
            sb(f"ncsf{s}", [8, D], F32)
            sb(f"zlast{s}", [8, 1], F32)
        sb("xnf", [128, D], F32)
        sb("bsc", [128, D], F32)
        sb("wstf", [128, 128], F32)
        sb("wst_h", [128, 128], BF16)
        for n in ["atf", "eat", "ebt", "zt", "lnzt", "ptt", "pzt",
                  "cwt", "eet", "gtf", "mtf"]:
            sb(n, [8, 128], F32)
        sb("bmptp", [9, 128], F32)
        sb("selc", [9, 1024], F32)
        sb("cwh", [8, 128], BF16)
        sb("eecol_h", [128, 8], BF16)
        sb("bmpcol", [128, 8], F32)
        sb("epend", [8, 1], F32)
        for n in ["xsum", "ssq", "mean", "m2", "var", "rstd", "mrstd",
                  "mx", "nmx", "sumg", "invg"]:
            sb(n, [128, 1], F32)
        sb("av", [128, 8], F32)
        sb("bq", [128, 8], F32)
        sb("gf", [128, 8], F32)
        sb("grf", [128, 8], F32)
        sb("midf", [128, 8], F32)
        sb("gbits", [128, 8], I16)
        sb("m1f", [64, 128], F32)
        sb("gff", [64, 128], F32)
        sb("g65_h", [65, 128], BF16)
        ps("t0", [128, 512])
        ps("t1", [128, 512])
        ps("pq", [128, 512])
        ps("pk", [128, 512])
        ps("pal", [128, 512])
        ps("pw", [128, 512])
        ps("pv", [128, 1024])

        E = pg.emit

        E("pool", lambda e: e.memset(T["idf"][:], 0.0), writes=[T["idf"]])
        E("pool", lambda e: e.affine_select(out=T["idf"][:], in_=T["idf"][:],
                                            compare_op=ALU.not_equal, fill=1.0, base=0,
                                            pattern=[[-1, 128]], channel_multiplier=1),
          reads=[T["idf"]], writes=[T["idf"]])
        E("pool", lambda e: e.memset(T["m0"][:], 0.0), writes=[T["m0"]])
        E("pool", lambda e: e.affine_select(out=T["m0"][:], in_=T["m0"][:],
                                            compare_op=ALU.is_ge, fill=-1e30, base=0,
                                            pattern=[[1, 128]], channel_multiplier=-1),
          reads=[T["m0"]], writes=[T["m0"]])
        E("pool", lambda e: e.memset(T["m0r"][:], 0.0), writes=[T["m0r"]])
        E("pool", lambda e: e.affine_select(
            out=T["m0r"].rearrange("p (h t) -> p h t", h=4),
            in_=T["m0r"].rearrange("p (h t) -> p h t", h=4),
            compare_op=ALU.is_ge, fill=-1e30, base=0,
            pattern=[[0, 4], [1, 128]], channel_multiplier=-1),
          reads=[T["m0r"]], writes=[T["m0r"]])
        E("pool", lambda e: e.memset(T["bmptp"][:], 1.0), writes=[T["bmptp"]])
        E("pool", lambda e: e.memset(T["selc"][0:8, :], 0.0), writes=[(T["selc"], "c")])
        E("pool", lambda e: e.affine_select(
            out=T["selc"][0:8, :].rearrange("p (a t) -> p a t", a=8),
            in_=T["selc"][0:8, :].rearrange("p (a t) -> p a t", a=8),
            compare_op=ALU.not_equal, fill=1.0, base=0,
            pattern=[[-1, 8], [0, 128]], channel_multiplier=1),
          reads=[(T["selc"], "c")], writes=[(T["selc"], "c")])
        E("pool", lambda e: e.memset(T["id8"][:], 0.0), writes=[T["id8"]])
        E("pool", lambda e: e.affine_select(out=T["id8"][:], in_=T["id8"][:],
                                            compare_op=ALU.not_equal, fill=1.0, base=0,
                                            pattern=[[-1, 8]], channel_multiplier=1),
          reads=[T["id8"]], writes=[T["id8"]])
        E("pool", lambda e: e.memset(T["repe"][:], 0.0), writes=[T["repe"]])
        E("pool", lambda e: e.affine_select(
            out=T["repe"].rearrange("p (a b) -> p a b", a=8),
            in_=T["repe"].rearrange("p (a b) -> p a b", a=8),
            compare_op=ALU.not_equal, fill=1.0, base=0,
            pattern=[[-1, 8], [0, 8]], channel_multiplier=1),
          reads=[T["repe"]], writes=[T["repe"]])
        E("pool", lambda e: e.memset(T["repr"][:], 0.0), writes=[T["repr"]])
        E("pool", lambda e: e.affine_select(
            out=T["repr"].rearrange("p (a b) -> p a b", a=8),
            in_=T["repr"].rearrange("p (a b) -> p a b", a=8),
            compare_op=ALU.not_equal, fill=1.0, base=0,
            pattern=[[0, 8], [-1, 8]], channel_multiplier=1),
          reads=[T["repr"]], writes=[T["repr"]])
        E("pool", lambda e: e.memset(T["ones2"][:], 1.0), writes=[T["ones2"]])
        E("pool", lambda e: e.memset(T["g65_h"][64:65, :], 1.0), writes=[(T["g65_h"], "b")])
        E("pool", lambda e: e.memset(T["rb2"][:], 0.0), writes=[T["rb2"]])
        for s in range(nseq):
            E("pool", lambda e, s=s: e.memset(T[f"nch{s}"][:], 0.0), writes=[T[f"nch{s}"]])
            E("pool", lambda e, s=s: e.memset(T[f"ncsf{s}"][:], 0.0), writes=[T[f"ncsf{s}"]])
            E("pool", lambda e, s=s: e.memset(T[f"zlast{s}"][:], 0.0), writes=[T[f"zlast{s}"]])

        tps = ["t0", "t1"]

        def emit_prep():
            widx = 0
            stgs = ["bsc", "xb2"]

            def prep_w(dw, dname, douts):
                nonlocal widx
                step = 1024 // douts
                for k0 in range(0, KCH, step):
                    stg = T[stgs[widx % 2]]
                    E(f"dma:wf{widx % 2}@act",
                      lambda e, dw=dw, k0=k0, stg=stg, step=step, douts=douts:
                      e.dma_start(
                          stg[:].rearrange("p (c j) -> p c j", c=step),
                          dw.rearrange("(c p) j -> p c j", p=128)[:, k0:k0 + step, :]
                          if step > 1 else dw[k0 * 128:(k0 + 1) * 128, :].unsqueeze(1)),
                      writes=[stg])
                    co = k0 * douts
                    E("act", lambda e, stg=stg, co=co, dname=dname:
                      e.activation(out=T[dname + "_h"][:, co:co + 1024], in_=stg[:],
                                   func=AF.Copy),
                      reads=[stg], writes=[T[dname + "_h"]])
                    widx += 1

            def prep_w16(dw, dname):
                nonlocal widx
                stg = T[stgs[widx % 2]]
                E(f"dma:wf{widx % 2}@act",
                  lambda e, dw=dw, stg=stg: e.dma_start(
                      stg[:].rearrange("p (c j) -> p c j", c=8)[:, :, 0:16],
                      dw.rearrange("(c p) j -> p c j", p=128)),
                  writes=[stg])
                E("act", lambda e, stg=stg, dname=dname:
                  e.activation(out=T[dname + "_h"][:],
                               in_=stg[:].rearrange("p (c j) -> p c j", c=8)[:, :, 0:16],
                               func=AF.Copy),
                  reads=[stg], writes=[T[dname + "_h"]])
                widx += 1

            prep_w(dwq, "wq", 512)
            prep_w(dwk, "wk", 512)
            prep_w(dwal, "wal", 512)
            prep_w(dwv, "wv", 1024)
            prep_w(dwb, "wb", 1024)
            prep_w(dwo, "wo", 1024)
            prep_w16(drml, "rm")
            E("dma:wf0@act", lambda e: e.dma_start(T["bsc"][0:64, :], dlb), writes=[T["bsc"]])
            E("act", lambda e: e.activation(out=T["lb_h"][0:64, :], in_=T["bsc"][0:64, :],
                                            func=AF.Copy),
              reads=[T["bsc"]], writes=[(T["lb_h"], "a")])
            E("dve", lambda e: e.tensor_tensor(T["lb_l"][0:64, :], T["bsc"][0:64, :],
                                               T["lb_h"][0:64, :], op=ALU.subtract),
              reads=[T["bsc"], (T["lb_h"], "a")], writes=[(T["lb_l"], "a")])
            E("dma:wf1@act", lambda e: e.dma_start(T["bsc"][64:65, :], dbb), writes=[T["bsc"]])
            E("dve", lambda e: e.tensor_copy(T["lb_h"][64:65, :], T["bsc"][64:65, :]),
              reads=[T["bsc"]], writes=[(T["lb_h"], "b")])
            E("dve", lambda e: e.tensor_tensor(T["lb_l"][64:65, :], T["bsc"][64:65, :],
                                               T["lb_h"][64:65, :], op=ALU.subtract),
              reads=[T["bsc"], (T["lb_h"], "b")], writes=[(T["lb_l"], "b")])
            E("dma:wf1@act", lambda e: e.dma_start(T["bsc"][0:1, 0:512], dwab), writes=[T["bsc"]])
            E("dve", lambda e: e.tensor_copy(T["vh_h"][0:1, 0:512], T["bsc"][0:1, 0:512]),
              reads=[T["bsc"]], writes=[T["vh_h"]])
            E("dve", lambda e: e.tensor_tensor(T["yt_h"][0:1, 0:512], T["bsc"][0:1, 0:512],
                                               T["vh_h"][0:1, 0:512], op=ALU.subtract),
              reads=[T["bsc"], T["vh_h"]], writes=[T["yt_h"]])
            E("dma:wr@act", lambda e: e.dma_start(T["wab2"][0:1, :], T["vh_h"][0:1, 0:512]),
              reads=[T["vh_h"]], writes=[T["wab2"]])
            E("dma:wr@act", lambda e: e.dma_start(T["wab2"][1:2, :], T["yt_h"][0:1, 0:512]),
              reads=[T["yt_h"]], writes=[T["wab2"]])
            E("dma:wf1@act", lambda e: e.dma_start(T["bsc"][0:1, 0:8], drb), writes=[T["bsc"]])
            E("dve", lambda e: e.tensor_copy(T["vh_h"][0:1, 0:8], T["bsc"][0:1, 0:8]),
              reads=[T["bsc"]], writes=[T["vh_h"]])
            E("dve", lambda e: e.tensor_tensor(T["yt_h"][0:1, 0:8], T["bsc"][0:1, 0:8],
                                               T["vh_h"][0:1, 0:8], op=ALU.subtract),
              reads=[T["bsc"], T["vh_h"]], writes=[T["yt_h"]])
            E("dma:wr@act", lambda e: e.dma_start(T["rb2"][0:1, 0:8], T["vh_h"][0:1, 0:8]),
              reads=[T["vh_h"]], writes=[T["rb2"]])
            E("dma:wr@act", lambda e: e.dma_start(T["rb2"][1:2, 0:8], T["yt_h"][0:1, 0:8]),
              reads=[T["yt_h"]], writes=[T["rb2"]])

        slots = [(s, ch) for ch in range(NCH) for s in range(nseq)]
        NS = len(slots)

        def load_slot(si):
            if si >= NS:
                return
            s, ch = slots[si]
            xt = T[f"xb{si % 3}"]
            E(f"dma:x{si % 3}@sp",
              lambda e, s=s, ch=ch, xt=xt: e.dma_start(xt[:], dx[s, ch * TC:(ch + 1) * TC, :]),
              writes=[xt])

        def xnth(k):
            return T["xnt_h"][:, k * 128:(k + 1) * 128]

        def proj_group(dest_ap, dest_res, wname, co_fn, nblk, extra=None,
                       extra_reads=()):
            wh = T[wname + "_h"]

            def g(e):
                last = None
                for k in range(KCH):
                    co = co_fn(k)
                    last = e.matmul(dest_ap, xnth(k), wh[:, co:co + nblk],
                                    start=(k == 0),
                                    stop=(k == KCH - 1 and extra is None))
                if extra is not None:
                    last = extra(e)
                return last
            reads = [T["xnt_h"], wh] + list(extra_reads)
            E("pe", g, reads=reads, writes=[dest_res])

        import os as _os
        TRB = _os.environ.get("KERN_TRBATCH", "1") == "1"
        PIPE = _os.environ.get("KERN_PIPE", "1") == "1"
        SBANKS = ("pal", "pw")
        WMASK_MM = _os.environ.get("KERN_WMASK", "dve") == "mm"

        def tr_batch(src, src_res, dh, dh_res, base):
            if not TRB:
                for k in range(KCH):
                    tp = T[tps[k % 2]]
                    E("pe", lambda e, k=k, tp=tp: e.transpose(
                        tp[:, 0:128], src[:, k * 128:(k + 1) * 128], T["idf"][:]),
                      reads=[src_res, T["idf"]], writes=[tp])
                    co = base + k * 128
                    E("act", lambda e, tp=tp, co=co: e.activation(
                        out=dh[:, co:co + 128], in_=tp[:, 0:128], func=AF.Copy),
                      reads=[tp], writes=[dh_res])
                return
            for half in range(2):
                tp = T[tps[half]]

                def tr4(e, half=half, tp=tp):
                    last = None
                    for j in range(4):
                        k = half * 4 + j
                        last = e.transpose(tp[:, j * 128:(j + 1) * 128],
                                           src[:, k * 128:(k + 1) * 128], T["idf"][:])
                    return last
                E("pe", tr4, reads=[src_res, T["idf"]], writes=[tp])
                co = base + half * 512
                E("act", lambda e, tp=tp, co=co: e.activation(
                    out=dh[:, co:co + 512], in_=tp[:], func=AF.Copy),
                  reads=[tp], writes=[(dh_res, f"h{half}")])

        def front(si):
            if si >= NS:
                return
            s, ch = slots[si]
            xt = T[f"xb{si % 3}"]
            load_slot(si + 1)
            E("dve", lambda e, xt=xt: e.tensor_reduce(T["xsum"][:], xt[:], axis=AX.X, op=ALU.add),
              reads=[xt], writes=[T["xsum"]])
            E("act", lambda e, xt=xt: e.activation(out=T["xnf"][:], in_=xt[:], func=AF.Square,
                                                   accum_out=T["ssq"][:]),
              reads=[xt], writes=[T["xnf"], T["ssq"]])
            E("dve", lambda e: e.tensor_scalar_mul(T["mean"][:], T["xsum"][:], 1.0 / D),
              reads=[T["xsum"]], writes=[T["mean"]])
            E("dve", lambda e: e.tensor_tensor(T["m2"][:], T["mean"][:], T["mean"][:], op=ALU.mult),
              reads=[T["mean"]], writes=[T["m2"]])
            E("dve", lambda e: e.scalar_tensor_tensor(T["var"][:], T["ssq"][:], 1.0 / D,
                                                      T["m2"][:], op0=ALU.mult, op1=ALU.subtract),
              reads=[T["ssq"], T["m2"]], writes=[T["var"]])
            E("dve", lambda e: e.tensor_scalar_add(T["var"][:], T["var"][:], 1e-5),
              reads=[T["var"]], writes=[T["var"]])
            E("act", lambda e: e.activation(out=T["rstd"][:], in_=T["var"][:], func=AF.Ln),
              reads=[T["var"]], writes=[T["rstd"]])
            E("act", lambda e: e.activation(out=T["rstd"][:], in_=T["rstd"][:], func=AF.Exp, scale=-0.5),
              reads=[T["rstd"]], writes=[T["rstd"]])
            E("dve", lambda e: e.tensor_tensor(T["mrstd"][:], T["mean"][:], T["rstd"][:], op=ALU.mult),
              reads=[T["mean"], T["rstd"]], writes=[T["mrstd"]])
            E("dve", lambda e, xt=xt: e.tensor_scalar(T["xnf"][:], xt[:], T["rstd"][:], T["mrstd"][:],
                                                      op0=ALU.mult, op1=ALU.subtract),
              reads=[xt, T["rstd"], T["mrstd"]], writes=[T["xnf"]])
            tr_batch(T["xnf"], T["xnf"], T["xnt_h"], T["xnt_h"], 0)

        def mid(si):
            s, ch = slots[si]
            xt = T[f"xb{si % 3}"]
            nch_, ncf = T[f"nch{s}"], T[f"ncsf{s}"]
            zlast = T[f"zlast{s}"]

            for part in range(2):
                def qmm(e, part=part):
                    last = None
                    for k in range(part * 4, part * 4 + 4):
                        co = k * 512
                        last = e.matmul(T["pq"][:], xnth(k), T["wq_h"][:, co:co + 512],
                                        start=(k == 0), stop=(k == KCH - 1))
                    return last
                E("pe", qmm,
                  reads=[(T["xnt_h"], f"h{part}"), T["wq_h"]],
                  writes=[T["pq"]])
            proj_group(T["pk"][:], T["pk"], "wk", lambda k: k * 512, 512)
            proj_group(T["t0"][:, 0:16], T["t0"], "rm", lambda k: k * 16, 16,
                       extra=lambda e: e.matmul(T["t0"][:, 0:16], T["ones2"][:], T["rb2"][:],
                                                start=False, stop=True),
                       extra_reads=[T["ones2"], T["rb2"]])
            proj_group(T["pal"][:], T["pal"], "wal", lambda k: k * 512, 512,
                       extra=lambda e: e.matmul(T["pal"][:], T["ones2"][:], T["wab2"][:],
                                                start=False, stop=True),
                       extra_reads=[T["ones2"], T["wab2"]])

            E("dve", lambda e: e.tensor_copy(T["xnf"][:, 0:512], T["pq"][:]),
              reads=[T["pq"]], writes=[(T["xnf"], "sa")])
            E("dve", lambda e: e.tensor_tensor(T["xnf"][:, 512:1024], T["xnf"][:, 0:512],
                                               T["pk"][:], op=ALU.mult),
              reads=[(T["xnf"], "sa"), T["pk"]], writes=[(T["xnf"], "sb")])
            E("dve", lambda e: e.tensor_reduce(
                T["bq"][:].unsqueeze(2),
                T["xnf"][:, 512:1024].rearrange("p (h w) -> p h w", h=H),
                axis=AX.X, op=ALU.add),
              reads=[(T["xnf"], "sb")], writes=[T["bq"]])

            E("dve", lambda e: e.tensor_reduce(T["nmx"][:], T["t0"][:, 0:8], axis=AX.X, op=ALU.max,
                                               negate=True),
              reads=[T["t0"]], writes=[T["nmx"]])
            E("act", lambda e: e.activation(out=T["gf"][:], in_=T["t0"][:, 0:8], func=AF.Exp,
                                            bias=T["nmx"][:]),
              reads=[T["t0"], T["nmx"]], writes=[T["gf"]])
            E("dve", lambda e: e.tensor_reduce(T["sumg"][:], T["gf"][:], axis=AX.X, op=ALU.add),
              reads=[T["gf"]], writes=[T["sumg"]])
            E("dve", lambda e: e.reciprocal(T["invg"][:], T["sumg"][:]),
              reads=[T["sumg"]], writes=[T["invg"]])
            E("dve", lambda e: e.tensor_scalar(T["gf"][:], T["gf"][:], T["invg"][:], None,
                                               op0=ALU.mult),
              reads=[T["gf"], T["invg"]], writes=[T["gf"]])
            E("dve", lambda e: e.tensor_copy(T["gbits"][:].bitcast(BF16), T["gf"][:]),
              reads=[T["gf"]], writes=[T["gbits"]])
            E("dve", lambda e: e.tensor_copy(T["grf"][:], T["gbits"][:].bitcast(BF16)),
              reads=[T["gbits"]], writes=[T["grf"]])
            E("dve", lambda e: e.tensor_scalar_mul(T["midf"][:], T["t0"][:, 8:16], 2.0),
              reads=[T["t0"]], writes=[T["midf"]])

            proj_group(T["pv"][:, 0:512], (T["pv"], "h0"), "wv", lambda k: k * 1024, 512)
            proj_group(T["pv"][:, 512:1024], (T["pv"], "h1"), "wv", lambda k: k * 1024 + 512, 512)

            E("act", lambda e: e.activation(out=T["xnf"][:, 0:512], in_=T["pal"][:], func=AF.Exp),
              reads=[T["pal"]], writes=[(T["xnf"], "sa")])
            E("act", lambda e: e.activation(out=T["xnf"][:, 0:512], in_=T["xnf"][:, 0:512],
                                            func=AF.Ln, bias=1.0),
              reads=[(T["xnf"], "sa")], writes=[(T["xnf"], "sa")])
            E("dve", lambda e: e.tensor_reduce(
                T["av"][:].unsqueeze(2),
                T["xnf"][:, 0:512].rearrange("p (h w) -> p h w", h=H),
                axis=AX.X, op=ALU.add, negate=True),
              reads=[(T["xnf"], "sa")], writes=[T["av"]])

            E("act", lambda e: e.activation(out=T["vh_h"][:, 0:512], in_=T["pv"][:, 0:512],
                                            func=AF.Copy),
              reads=[(T["pv"], "h0")], writes=[(T["vh_h"], "h0")])
            E("dve", lambda e: e.tensor_copy(T["vh_h"][:, 512:1024], T["pv"][:, 512:1024]),
              reads=[(T["pv"], "h1")], writes=[(T["vh_h"], "h1")])
            E("pe", lambda e: e.transpose(T["t1"][0:8, 0:128], T["av"][:], T["idf"][:]),
              reads=[T["av"], T["idf"]], writes=[T["t1"]])
            E("dve", lambda e: e.tensor_copy(T["atf"][:], T["t1"][0:8, 0:128]),
              reads=[T["t1"]], writes=[T["atf"]])
            E("act", lambda e: e.activation(out=T["eat"][:], in_=T["t1"][0:8, 0:128], func=AF.Exp),
              reads=[T["t1"]], writes=[T["eat"]])
            E("pe", lambda e: e.transpose(T["t0"][0:8, 0:128], T["bq"][:], T["idf"][:]),
              reads=[T["bq"], T["idf"]], writes=[T["t0"]])
            E("act", lambda e: e.activation(out=T["ebt"][:], in_=T["t0"][0:8, 0:128], func=AF.Exp,
                                            scale=0.125),
              reads=[T["t0"]], writes=[T["ebt"]])
            E("pe", lambda e: e.transpose(T["t1"][0:8, 128:256], T["grf"][:], T["idf"][:]),
              reads=[T["grf"], T["idf"]], writes=[T["t1"]])
            E("dve", lambda e: e.tensor_copy(T["gtf"][:], T["t1"][0:8, 128:256]),
              reads=[T["t1"]], writes=[T["gtf"]])
            E("pe", lambda e: e.transpose(T["t0"][0:8, 128:256], T["midf"][:], T["idf"][:]),
              reads=[T["midf"], T["idf"]], writes=[T["t0"]])
            E("dve", lambda e: e.tensor_copy(T["mtf"][:], T["t0"][0:8, 128:256]),
              reads=[T["t0"]], writes=[T["mtf"]])
            E("pe", lambda e: e.matmul(T["t1"][0:64, 256:384], T["repe"][:], T["gtf"][:],
                                       start=True, stop=True),
              reads=[T["repe"], T["gtf"]], writes=[T["t1"]])
            E("dve", lambda e: e.tensor_copy(T["m1f"][:], T["t1"][0:64, 256:384]),
              reads=[T["t1"]], writes=[T["m1f"]])
            E("pe", lambda e: e.matmul(T["t0"][0:64, 256:384], T["repr"][:], T["mtf"][:],
                                       start=True, stop=True),
              reads=[T["repr"], T["mtf"]], writes=[T["t0"]])
            E("dve", lambda e: e.tensor_tensor(T["gff"][:], T["m1f"][:], T["t0"][0:64, 256:384],
                                               op=ALU.mult),
              reads=[T["m1f"], T["t0"]], writes=[T["gff"]])
            E("dve", lambda e: e.tensor_copy(T["g65_h"][0:64, :], T["gff"][:]),
              reads=[T["gff"]], writes=[(T["g65_h"], "a")])

            for hf in range(2):
                def umm(e, hf=hf):
                    dst = T["pv"][:, hf * 512:(hf + 1) * 512]
                    for k in range(KCH):
                        co = k * 1024 + hf * 512
                        e.matmul(dst, xnth(k), T["wb_h"][:, co:co + 512],
                                 start=(k == 0), stop=False)
                    sl = slice(hf * 512, (hf + 1) * 512)
                    e.matmul(dst, T["g65_h"][:], T["lb_h"][:, sl], start=False, stop=False)
                    return e.matmul(dst, T["g65_h"][:], T["lb_l"][:, sl], start=False, stop=True)
                E("pe", umm,
                  reads=[T["xnt_h"], T["wb_h"],
                         T["g65_h"], T["lb_h"], T["lb_l"]],
                  writes=[(T["pv"], f"h{hf}")])

            E("dve", lambda e: e.tensor_tensor_scan(T["ptt"][:], T["atf"][:], T["atf"][:], 0.0,
                                                    op0=ALU.add, op1=ALU.bypass),
              reads=[T["atf"]], writes=[T["ptt"]])
            E("dve", lambda e, zlast=zlast: e.tensor_tensor_scan(
                T["zt"][:], T["eat"][:], T["ebt"][:], zlast[:], op0=ALU.mult, op1=ALU.add),
              reads=[T["eat"], T["ebt"], zlast], writes=[T["zt"]])
            E("dve", lambda e, zlast=zlast: e.tensor_copy(zlast[:], T["zt"][:, 127:128]),
              reads=[T["zt"]], writes=[zlast])
            E("act", lambda e: e.activation(out=T["lnzt"][:], in_=T["zt"][:], func=AF.Ln),
              reads=[T["zt"]], writes=[T["lnzt"]])
            E("pool", lambda e: e.tensor_tensor(T["pzt"][:], T["ptt"][:], T["lnzt"][:],
                                                op=ALU.subtract),
              reads=[T["ptt"], T["lnzt"]], writes=[T["pzt"]])
            E("act", lambda e: e.activation(out=T["cwt"][:], in_=T["pzt"][:], func=AF.Exp),
              reads=[T["pzt"]], writes=[T["cwt"]])
            E("pool", lambda e: e.tensor_copy(T["cwh"][:], T["cwt"][:]),
              reads=[T["cwt"]], writes=[T["cwh"]])
            E("dma:pz@sp", lambda e: e.dma_start(dpzs, T["pzt"][:]),
              reads=[T["pzt"]], writes=["pzs"])
            E("dma:pz@sp", lambda e: e.dma_start(
                T["selc"][8:9, :],
                dpzs.rearrange("h t -> (h t)").unsqueeze(0)),
              reads=["pzs"], writes=[(T["selc"], "pz")])
            E("dve", lambda e: e.scalar_tensor_tensor(T["bmptp"][0:8, :], T["t0"][0:8, 0:128],
                                                      0.125, T["ptt"][:], op0=ALU.mult,
                                                      op1=ALU.subtract),
              reads=[T["t0"], T["ptt"]], writes=[(T["bmptp"], "b")])
            E("act", lambda e: e.activation(out=T["eet"][:], in_=T["bmptp"][0:8, :], func=AF.Exp,
                                            bias=T["ptt"][:, 127:128]),
              reads=[(T["bmptp"], "b"), T["ptt"]], writes=[T["eet"]])
            E("act", lambda e: e.activation(out=T["epend"][:], in_=T["ptt"][:, 127:128],
                                            func=AF.Exp),
              reads=[T["ptt"]], writes=[T["epend"]])
            E("pe", lambda e: e.transpose(T["t0"][0:128, 0:8], T["eet"][:], T["id8"][:]),
              reads=[T["eet"], T["id8"]], writes=[T["t0"]])
            E("act", lambda e: e.activation(out=T["eecol_h"][:], in_=T["t0"][:, 0:8], func=AF.Copy),
              reads=[T["t0"]], writes=[T["eecol_h"]])

            for hb in range(2):
                Sb = T["pq"] if hb == 0 else T["pk"]
                E("pe", lambda e, Sb=Sb, hb=hb: e.matmul(
                    Sb[:, 0:512], T["bmptp"][:],
                    T["selc"][:, hb * 512:(hb + 1) * 512], start=True, stop=True),
                  reads=[T["bmptp"], (T["selc"], "c"), (T["selc"], "pz")], writes=[Sb])
            for hb in range(2):
                Sb = T["pq"] if hb == 0 else T["pk"]
                E("dve", lambda e, Sb=Sb, hb=hb: e.tensor_tensor(
                    T["bsc"][:, hb * 512:(hb + 1) * 512], Sb[:], T["m0r"][:], op=ALU.add),
                  reads=[Sb, T["m0r"]], writes=[(T["bsc"], "wa" if hb == 0 else "wb")])
            E("act", lambda e: e.activation(out=T["bsc"][:, 0:512], in_=T["bsc"][:, 0:512],
                                            func=AF.Exp),
              reads=[(T["bsc"], "wa")], writes=[(T["bsc"], "wa")])
            E("act", lambda e: e.activation(out=T["bsc"][:, 512:1024], in_=T["bsc"][:, 512:1024],
                                            func=AF.Exp),
              reads=[(T["bsc"], "wb")], writes=[(T["bsc"], "wb")])
            E("dve", lambda e: e.tensor_reduce(
                T["wstf"][:].unsqueeze(2),
                T["bsc"][:].rearrange("p (a t) -> p t a", a=8),
                axis=AX.X, op=ALU.add),
              reads=[(T["bsc"], "wa"), (T["bsc"], "wb")], writes=[T["wstf"]])
            E("act", lambda e: e.activation(out=T["wst_h"][:], in_=T["wstf"][:], func=AF.Copy),
              reads=[T["wstf"]], writes=[T["wst_h"]])

            for hf, bank in [(0, "t0"), (1, "t1")]:
                def nemm(e, hf=hf, bank=bank):
                    dst = T[bank][0:8, 0:512]
                    sl = slice(hf * 512, (hf + 1) * 512)
                    return e.matmul(dst, T["eecol_h"][:], T["vh_h"][:, sl],
                                    start=True, stop=True)
                E("pe", nemm,
                  reads=[T["eecol_h"], T["vh_h"]],
                  writes=[T[bank]])

            for bi, bank in [(0, "pal"), (1, "pw")]:
                def ycar(e, bi=bi, bank=bank):
                    last = None
                    for i in range(4):
                        db = bi * 4 + i
                        dst = T[bank][:, i * 128:(i + 1) * 128]
                        sl = slice(db * 128, (db + 1) * 128)
                        last = e.matmul(dst, nch_[0:8, sl], T["cwh"][:],
                                        start=(i == 0), stop=False)
                    return last
                E("pe", ycar,
                  reads=[T["cwh"], nch_], writes=[T[bank]])
            for bi, bank in [(0, "pal"), (1, "pw")]:
                def yws(e, bi=bi, bank=bank):
                    last = None
                    for i in range(4):
                        db = bi * 4 + i
                        dst = T[bank][:, i * 128:(i + 1) * 128]
                        sl = slice(db * 128, (db + 1) * 128)
                        last = e.matmul(dst, T["vh_h"][:, sl], T["wst_h"][:], start=False,
                                        stop=(i == 3))
                    return last
                E("pe", yws,
                  reads=[T["wst_h"], T["vh_h"]], writes=[T[bank]])

            E("act", lambda e: e.activation(
                out=T["yt_h"][:, 0:512], in_=T["pal"][:], func=AF.Copy),
              reads=[T["pal"]], writes=[(T["yt_h"], "h0")])
            E("dve", lambda e: e.tensor_copy(T["yt_h"][:, 512:1024], T["pw"][:]),
              reads=[T["pw"]], writes=[(T["yt_h"], "h1")])

            for hf, bank in [(0, SBANKS[0]), (1, SBANKS[1])]:
                for part in range(2):
                    def smm(e, hf=hf, bank=bank, part=part):
                        dst = T[bank][:]
                        last = None
                        for k in range(part * 4, part * 4 + 4):
                            co = k * 1024 + hf * 512
                            last = e.matmul(dst, T["yt_h"][:, k * 128:(k + 1) * 128],
                                            T["wo_h"][:, co:co + 512],
                                            start=(k == 0), stop=(k == KCH - 1))
                        return last
                    E("pe", smm,
                      reads=[(T["yt_h"], f"h{part}"), T["wo_h"]],
                      writes=[T[bank]])

            for hf, bank in [(0, "t0"), (1, "t1")]:
                E("dve", lambda e, hf=hf, bank=bank, ncf=ncf: e.scalar_tensor_tensor(
                    ncf[:, hf * 512:(hf + 1) * 512], ncf[:, hf * 512:(hf + 1) * 512],
                    T["epend"][:], T[bank][0:8, 0:512], op0=ALU.mult, op1=ALU.add),
                  reads=[(ncf, f"h{hf}"), T["epend"], T[bank]], writes=[(ncf, f"h{hf}")])
            E("act", lambda e, nch_=nch_, ncf=ncf: e.activation(out=nch_[:], in_=ncf[:],
                                                                func=AF.Copy),
              reads=[(ncf, "h0"), (ncf, "h1")], writes=[nch_])


        def back(si):
            s, ch = slots[si]
            xt = T[f"xb{si % 3}"]
            E("act", lambda e: e.activation(out=T["bsc"][:], in_=T["pv"][:], func=AF.Silu),
              reads=[(T["pv"], "h0"), (T["pv"], "h1")], writes=[T["bsc"]])
            E("dve", lambda e: e.tensor_tensor(T["bsc"][:, 0:512], T["bsc"][:, 0:512],
                                               T[SBANKS[0]][:], op=ALU.mult),
              reads=[T["bsc"], T[SBANKS[0]]], writes=[T["bsc"]])
            E("dve", lambda e: e.tensor_tensor(T["bsc"][:, 512:1024], T["bsc"][:, 512:1024],
                                               T[SBANKS[1]][:], op=ALU.mult),
              reads=[T["bsc"], T[SBANKS[1]]], writes=[T["bsc"]])
            E("pool", lambda e, xt=xt: e.tensor_tensor(xt[:], T["bsc"][:], xt[:], op=ALU.add),
              reads=[T["bsc"], xt], writes=[xt])
            E(f"dma:o{si % 3}@sp",
              lambda e, s=s, ch=ch, xt=xt: e.dma_start(dout[s, ch * TC:(ch + 1) * TC, :], xt[:]),
              reads=[xt])

        load_slot(0)
        front(0)
        emit_prep()
        for si in range(NS):
            mid(si)
            front(si + 1)
            back(si)

        pg.finalize(es, end_waits=pg.barrier_token())
    n_instr = sum(len(v) for v in pg.ops.values())
    return nc, {"n_instr": n_instr, "n_waits": pg.n_waits, "n_bumps": pg.n_bumps}


def stage_inputs(inputs, core, n_cores=8):
    x = inputs["L0_attn_x"]
    B = x.shape[0]
    per = B // n_cores
    return {
        "x": np.ascontiguousarray(x[core * per:(core + 1) * per]),
        "wq": np.ascontiguousarray(inputs["L0_attn_W_q"].T),
        "wk": np.ascontiguousarray(inputs["L0_attn_W_k"].T),
        "wal": np.ascontiguousarray(inputs["L0_attn_W_alpha_w"].T),
        "wv": np.ascontiguousarray(inputs["L0_attn_W_v"].T),
        "wo": np.ascontiguousarray(inputs["L0_attn_W_o"].T),
        "wb": np.ascontiguousarray(inputs["L0_ffn_base_w"].T),
        "wab": np.ascontiguousarray(inputs["L0_attn_W_alpha_b"].reshape(1, 512)),
        "bb": np.ascontiguousarray(inputs["L0_ffn_base_b"].reshape(1, D)),
        "rml": np.ascontiguousarray(
            np.concatenate([inputs["L0_ffn_router_w"].T,
                            inputs["L0_ffn_lora_A"]], axis=1)),
        "lb": np.ascontiguousarray(inputs["L0_ffn_lora_B"].reshape(64, D)),
        "rb": np.ascontiguousarray(inputs["L0_ffn_router_b"].reshape(1, 8)),
    }


_NC_CACHE = {}


def kernel(**inputs):
    from concourse.bass_utils import run_bass_kernel_spmd
    n_cores = 8
    if "nc" not in _NC_CACHE:
        _NC_CACHE["nc"] = build_nc(S=2048, nseq=2)[0]
    nc = _NC_CACHE["nc"]
    in_maps = [stage_inputs(inputs, c, n_cores=n_cores) for c in range(n_cores)]
    res = run_bass_kernel_spmd(nc, in_maps, core_ids=list(range(n_cores)))
    out = np.concatenate([r["out"] for r in res.results], axis=0)
    return np.ascontiguousarray(out.reshape(16, 2048, 1024), dtype=np.float32)
